# version 12
# baseline (speedup 1.0000x reference)
import numpy as np
import ml_dtypes
from collections import defaultdict
from contextlib import ExitStack

import concourse.bass as bass
import concourse.mybir as mybir
from concourse.bass_utils import run_bass_kernel_spmd

F32 = mybir.dt.float32
BF16 = mybir.dt.bfloat16
ALU = mybir.AluOpType
ACT = mybir.ActivationFunctionType

NCORES = 8
TC = 1024
C = 2048
NCB = 16
TH = 512
FF = 8192
NFB = 64
NQB = 16
LN_EPS = 1e-5

ENGINES = ("sync", "gpsimd", "tensor", "vector", "scalar")


class Sched:

    def __init__(self):
        self.ops = {e: [] for e in ENGINES}
        self.cnt = defaultdict(int)

    def op(self, eng, fn, wait=(), inc=None, drain=True, tag=""):
        for s, v in wait:
            assert isinstance(v, int), f"wait {s} is {v!r} at {tag}"
        after = None
        if inc is not None:
            s, a = inc
            self.cnt[s] += a
            after = self.cnt[s]
        self.ops[eng].append((tuple(wait), fn, inc, drain, tag))
        return after

    def dma(self, eng, fn, wait=(), sem="LD", tag=""):
        return self.op(eng, fn, wait=wait, inc=(sem, 16), drain=False, tag=tag)

    def mm(self, eng, fn, wait=(), inc=None, tag=""):
        return self.op(eng, fn, wait=wait, inc=inc, drain=False, tag=tag)


def build_program(stage="full"):
    nc = bass.Bass("TRN2", target_bir_lowering=False)

    d_x = nc.dram_tensor("x", [C, TC], F32, kind="ExternalInput").ap()
    d_xprev = nc.dram_tensor("xprev", [128, NCB], F32, kind="ExternalInput").ap()
    d_wkb = nc.dram_tensor("wkb", [NCB, 128, C], F32, kind="ExternalInput").ap()
    d_wvb = nc.dram_tensor("wvb", [NCB, 128, C], F32, kind="ExternalInput").ap()
    d_wrb = nc.dram_tensor("wrb", [NCB, 128, C], F32, kind="ExternalInput").ap()
    d_wob = nc.dram_tensor("wob", [NCB, 128, C], F32, kind="ExternalInput").ap()
    d_w1s = nc.dram_tensor("w1s", [8, 128, C], F32, kind="ExternalInput").ap()
    d_w2s = nc.dram_tensor("w2s", [8, 128, C], F32, kind="ExternalInput").ap()
    d_muk = nc.dram_tensor("muk", [128, NCB], F32, kind="ExternalInput").ap()
    d_mur = nc.dram_tensor("mur", [128, NCB], F32, kind="ExternalInput").ap()
    d_muv = nc.dram_tensor("muv", [128, NCB], F32, kind="ExternalInput").ap()
    d_uv = nc.dram_tensor("uv", [128, NCB], F32, kind="ExternalInput").ap()
    d_wv = nc.dram_tensor("wv", [128, NCB], F32, kind="ExternalInput").ap()
    d_b1 = nc.dram_tensor("b1", [128, NFB], F32, kind="ExternalInput").ap()
    d_b2 = nc.dram_tensor("b2", [128, NCB], F32, kind="ExternalInput").ap()
    d_mask = nc.dram_tensor("mask", [128, NCORES], F32, kind="ExternalInput").ap()
    d_out = nc.dram_tensor("out", [C, TC], F32, kind="ExternalOutput").ap()

    d_xr = nc.dram_tensor("xr_scr", [C, TC], BF16).ap()
    d_q1in = nc.dram_tensor("q1in", [8, 128, C], mybir.dt.int8).ap()
    d_q2in = nc.dram_tensor("q2in", [8, 128, C], mybir.dt.int8).ap()
    d_q1all = nc.dram_tensor("q1all", [NFB, 128, C], mybir.dt.int8).ap()
    d_q2all = nc.dram_tensor("q2all", [NFB, 128, C], mybir.dt.int8).ap()
    d_sar_in = nc.dram_tensor("sar_in", [1, 2], F32).ap()
    d_sar_out = nc.dram_tensor("sar_out", [1, 2], F32).ap()
    d_wbf = nc.dram_tensor("wbf", [2 * NCB, 128, C], BF16).ap()
    d_q1bf = nc.dram_tensor("q1bf", [NFB, 128, C], BF16).ap()
    d_q2bf = nc.dram_tensor("q2bf", [NFB, 128, C], BF16).ap()
    d_sums_in = nc.dram_tensor("sums_in", [128, 32], F32).ap()
    d_sums_out = nc.dram_tensor("sums_out", [128 * NCORES, 32], F32).ap()

    dbg = {}
    if stage not in ("full",):
        for i in range(2):
            dbg[i] = nc.dram_tensor(f"dbg{i}", [C, TC], F32, kind="ExternalOutput").ap()
        dbg["s"] = nc.dram_tensor("dbgs", [128, 64], F32, kind="ExternalOutput").ap()

    sch = Sched()
    cnt = sch.cnt

    with ExitStack() as ctx:
        def sb(name, shape, dt=F32, st=None):
            return (st or ctx).enter_context(nc.sbuf_tensor(name, shape, dt))

        def psum(name, shape):
            return ctx.enter_context(nc.psum_tensor(name, shape, F32))

        t_muk = sb("t_muk", [128, NCB])
        t_amk = sb("t_amk", [128, NCB])
        t_amr = sb("t_amr", [128, NCB])
        t_amv = sb("t_amv", [128, NCB])
        t_xprev = sb("t_xprev", [128, NCB])
        t_u = sb("t_u", [128, NCB])
        t_w = sb("t_w", [128, NCB])
        t_weff = sb("t_weff", [128, NCB])
        t_dcol = sb("t_dcol", [128, NCB])
        t_e1k = sb("t_e1k", [128, NCB])
        t_e1km1 = sb("t_e1km1", [128, NCB])
        t_acol = sb("t_acol", [128, NCB])
        t_bcol = sb("t_bcol", [128, NCB])
        t_dm = sb("t_dm", [128, NCB])
        t_hs = sb("t_hs", [128, NCB])
        t_sums = sb("t_sums", [128, 32])
        t_gath = sb("t_gath", [128, NCORES, 32])
        t_maskt = sb("t_maskt", [128, NCORES])
        t_b1 = sb("t_b1", [128, NFB])
        t_b2 = sb("t_b2", [128, NCB])
        t_cons = sb("t_cons", [128, 16])
        t_onebf = sb("t_onebf", [128, 1], BF16)
        t_onef = sb("t_onef", [128, 1], F32)
        t_onerow = sb("t_onerow", [1, 128], F32)
        t_qsc = sb("t_qsc", [1, 48])
        t_tau = sb("t_tau", [128, 4])
        bfA = sb("bfA", [128, NCB, TC], BF16)
        bfB = sb("bfB", [128, NCB, TC], BF16)
        wpool = sb("wpool", [128, 4, NCB, 128], BF16)
        xin = sb("xin", [128, 2, TC], F32)

        (CONS_ZERO, CONS_ONE, CONS_NEG1, CONS_HALF, CONS_NHALF, CONS_INV2048,
         CONS_EPS, CONS_NQF, CONS_NC0, CONS_C0, CONS_INV224, CONS_P07,
         CONS_BIAS) = range(13)

        def cons(i):
            return t_cons[:, i:i + 1]

        big = [psum(f"big{i}", [128, TC]) for i in range(4)]

        stA2 = ExitStack()
        NUM = sb("NUM", [128, NCB, TC], BF16, st=stA2)
        DEN = sb("DEN", [128, NCB, TC], BF16, st=stA2)
        stA1 = ExitStack()
        T6 = sb("T6", [128, 6, TC], F32, st=stA1)
        T3 = sb("T3", [128, 2, TC], F32, st=stA1)
        t_iota = sb("t_iota", [128, TC], F32, st=stA1)
        xrring = sb("xrring", [128, 1, TC], BF16, st=stA1)
        qring = sb("qring", [128, 1024], F32, st=stA1)
        qout = sb("qout", [128, 1024], BF16, st=stA1)
        qp2 = sb("qp2", [128, 1024], mybir.dt.int8, st=stA1)


        def ld(dst, src, tag):
            return sch.dma("sync", lambda e, d=dst, s=src: e.dma_start(out=d, in_=s),
                           sem="LS", tag=tag)

        ld(t_muk[:, :], d_muk, "muk")
        ld(t_amr[:, :], d_mur, "mur")
        ld(t_amv[:, :], d_muv, "muv")
        ld(t_xprev[:, :], d_xprev, "xprev")
        ld(t_u[:, :], d_uv, "u")
        ld(t_w[:, :], d_wv, "w")
        ld(t_b1[:, :], d_b1, "b1")
        ld(t_b2[:, :], d_b2, "b2")
        lx_setup = ld(t_maskt[:, :], d_mask, "mask")

        def gset(e):
            e.memset(t_cons[:, CONS_ZERO:CONS_ZERO + 1], 0.0)
            e.memset(t_cons[:, CONS_ONE:CONS_ONE + 1], 1.0)
            e.memset(t_cons[:, CONS_NEG1:CONS_NEG1 + 1], -1.0)
            e.memset(t_cons[:, CONS_HALF:CONS_HALF + 1], 0.5)
            e.memset(t_cons[:, CONS_NHALF:CONS_NHALF + 1], -0.5)
            e.memset(t_cons[:, CONS_INV2048:CONS_INV2048 + 1], 1.0 / 2048.0)
            e.memset(t_cons[:, CONS_EPS:CONS_EPS + 1], LN_EPS)
            e.memset(t_cons[:, CONS_NQF:CONS_NQF + 1], -0.7 / (8192.0 * 2048.0))
            e.memset(t_cons[:, CONS_NC0:CONS_NC0 + 1], -0.796875)
            e.memset(t_cons[:, CONS_C0:CONS_C0 + 1], 0.796875)
            e.memset(t_cons[:, CONS_INV224:CONS_INV224 + 1], 1.0 / 16777216.0)
            e.memset(t_cons[:, CONS_P07:CONS_P07 + 1], 0.7)
            e.memset(t_cons[:, CONS_BIAS:CONS_BIAS + 1], 1.0 + 2.0 ** -23)
            e.memset(t_onebf[:, :], 1.0)
            e.memset(t_onef[:, :], 1.0)
            e.memset(t_onerow[:, :], 1.0)
            e.memset(t_acol[:, :], 0.0)
            e.memset(t_bcol[:, :], 0.0)
            return e.iota(t_iota[:, :], pattern=[[1, TC]], base=0,
                          channel_multiplier=0, allow_small_or_imprecise_dtypes=True)
        ms_gset = sch.op("gpsimd", gset, inc=("GS", 1), tag="setup")

        ms_ew = sch.op("scalar",
                       lambda e: e.activation(t_weff[:, :], t_w[:, :], ACT.Exp),
                       wait=[("LS", lx_setup), ("GS", ms_gset)],
                       inc=("SS", 1), tag="exp_w")
        ms_weff = sch.op("vector",
                         lambda e: e.tensor_scalar_mul(t_weff[:, :], t_weff[:, :],
                                                       cons(CONS_NEG1)),
                         wait=[("SS", ms_ew)], inc=("VS", 1), tag="weff")
        ms_dcs = sch.op("scalar",
                        lambda e: e.activation(t_dcol[:, :], t_weff[:, :], ACT.Exp),
                        wait=[("VS", ms_weff)], inc=("SS", 1), tag="dcol")
        ms_sc1024 = sch.op("vector",
                           lambda e: e.tensor_scalar_mul(t_e1k[:, :], t_weff[:, :],
                                                         1024.0),
                           wait=[("VS", ms_weff)], inc=("VS", 1), tag="w1024")
        ms_e1k = sch.op("scalar",
                        lambda e: e.activation(t_e1k[:, :], t_e1k[:, :], ACT.Exp),
                        wait=[("VS", ms_sc1024)], inc=("SS", 1), tag="e1k")
        ms_e1km1 = sch.op("vector",
                          lambda e: e.tensor_scalar(t_e1km1[:, :], t_e1k[:, :],
                                                    cons(CONS_ONE), None, ALU.subtract),
                          wait=[("SS", ms_e1k)], inc=("VS", 1), tag="e1km1")
        ms_amk = sch.op("vector",
                        lambda e: e.tensor_scalar(t_amk[:, :], t_muk[:, :],
                                                  cons(CONS_ONE), None, ALU.subtract),
                        inc=("VS", 1), tag="amk")
        ms_amr = sch.op("vector",
                        lambda e: e.tensor_scalar(t_amr[:, :], t_amr[:, :],
                                                  cons(CONS_ONE), None, ALU.subtract),
                        inc=("VS", 1), tag="amr")
        ms_amv = sch.op("vector",
                        lambda e: e.tensor_scalar(t_amv[:, :], t_amv[:, :],
                                                  cons(CONS_ONE), None, ALU.subtract),
                        inc=("VS", 1), tag="amv")
        ms_setupv = max(ms_amk, ms_amr, ms_amv, ms_e1km1)

        def stream_blk(dsrc, blkidx, slot, waits, tag):
            sem = f"LW{slot}"
            val = sch.dma(
                "gpsimd",
                lambda e, d=dsrc, b=blkidx, s=slot: e.dma_start(
                    out=wpool[:, s].rearrange("p a b -> p (a b)"),
                    in_=d[b]),
                wait=waits, sem=sem, tag=tag)
            return (sem, val)

        qsrc = [( (d_w1s if i < 16 else d_w2s)[(i % 16) // 2]
                  [:, 1024 * ((i % 16) % 2):1024 * ((i % 16) % 2 + 1)] )
                for i in range(2 * NQB)]
        lw_idx = {}
        for cb in range(2):
            lw_idx[(cb, "k")] = stream_blk(d_wkb, cb, 2 * cb, [], f"wk{cb}")
            lw_idx[(cb, "v")] = stream_blk(d_wvb, cb, 2 * cb + 1, [], f"wv{cb}")

        gp_red = []
        ms_qst = {}
        for i in range(2 * NQB):
            qldw = [("SS", gp_red[-1])] if i > 0 else [("GS", ms_gset)]
            qld = sch.dma(
                "gpsimd",
                lambda e, s=qsrc[i]: e.dma_start(out=qring[:, :], in_=s),
                wait=qldw, sem="LQ", tag=f"qld{i}")
            qa32 = T3[:, 0, :]
            qs32 = T6[:, 4, :]
            absw = [("LQ", qld)]
            if i > 0:
                absw.append(("TS", ms_qst[i - 1]))

            def qabs(e, qa32=qa32, qs32=qs32):
                e.activation(qa32, qring[:, :], ACT.Abs)
                e.drain()
                return e.activation(qs32, qa32, ACT.Identity,
                                    bias=cons(CONS_NC0))
            gp_red.append(sch.op("scalar", qabs, wait=absw, inc=("SS", 1),
                                 tag=f"qabs{i}"))
            pst = big[2] if i < 16 else big[3]
            for q in range(2):
                ms_qst[i] = sch.mm(
                    "tensor",
                    lambda e, q=q, pst=pst, i=i, qs32=qs32: e.matmul(
                        pst[0:1, 0:TH], t_onef[:, 0:1],
                        qs32[:, TH * q:TH * (q + 1)],
                        start=(i % 16 == 0 and q == 0),
                        stop=(i % 16 == 15 and q == 1), skip_group_check=True),
                    wait=[("SS", gp_red[i])] if q == 0 else (),
                    inc=("TS", 1) if q == 1 else None, tag=f"qst{i}")
        ms_qstats = ms_qst[2 * NQB - 1]

        x_ld = []
        ms_mix = []
        for j in range(NCB):
            w = [("VS", ms_mix[j - 2])] if j >= 2 else []
            x_ld.append(sch.dma(
                "sync", lambda e, j=j: e.dma_start(out=xin[:, j % 2, :],
                                                   in_=d_x[128 * j:128 * (j + 1), :]),
                wait=w, sem=f"LX{j % 2}", tag=f"xld{j}"))
            xi = xin[:, j % 2, :]
            dd = T6[:, 0, :]

            def mix(e, j=j, xi=xi, dd=dd):
                e.tensor_sub(dd[:, 0:1], xi[:, 0:1], t_xprev[:, j:j + 1])
                e.drain()
                e.tensor_sub(dd[:, 1:TC], xi[:, 1:TC], xi[:, 0:TC - 1])
                e.drain()
                e.scalar_tensor_tensor(bfA[:, j, :], dd, t_amk[:, j:j + 1], xi,
                                       ALU.mult, ALU.add)
                e.drain()
                e.scalar_tensor_tensor(xrring[:, 0, :], dd, t_amr[:, j:j + 1], xi,
                                       ALU.mult, ALU.add)
                e.drain()
                return e.scalar_tensor_tensor(bfB[:, j, :], dd, t_amv[:, j:j + 1], xi,
                                              ALU.mult, ALU.add)
            mixw = [(f"LX{j % 2}", x_ld[j]), ("VS", ms_setupv)]
            if j >= 1:
                mixw.append(("XR", 16 * j))
            ms_mix.append(sch.op("vector", mix, wait=mixw, inc=("VS", 1),
                                 tag=f"mix{j}"))
            sch.dma("sync", lambda e, j=j: e.dma_start(
                out=d_xr[128 * j:128 * (j + 1), :], in_=xrring[:, 0, :]),
                wait=[("VS", ms_mix[j])], sem="XR", tag=f"xrst{j}")
        ms_mix_done = ms_mix[-1]

        def qrows(e):
            for mi, pst in ((0, big[2]), (1, big[3])):
                e.tensor_reduce(t_qsc[0:1, 16:48],
                                pst[0:1, 0:TH].rearrange("p (a b) -> p a b", a=32),
                                mybir.AxisListType.X, ALU.add)
                e.drain()
                e.tensor_reduce(t_qsc[0:1, mi:mi + 1], t_qsc[0:1, 16:48],
                                mybir.AxisListType.X, ALU.add)
                e.drain()
            return e.tensor_copy(t_qsc[0:1, 2:4], t_qsc[0:1, 0:2])
        ms_qrow = sch.op("vector", qrows, wait=[("TS", ms_qstats)],
                         inc=("VS", 1), tag="qrow")
        ms_sardma = sch.dma("sync",
                            lambda e: e.dma_start(out=d_sar_in, in_=t_qsc[0:1, 0:2]),
                            wait=[("VS", ms_qrow)], sem="SB", tag="sar_out")
        ms_ar = sch.op("gpsimd",
                       lambda e: e.collective_compute(
                           "AllReduce", ALU.add, replica_groups=[list(range(NCORES))],
                           ins=[d_sar_in], outs=[d_sar_out]),
                       wait=[("SB", ms_sardma)], inc=("CC", 1), drain=False,
                       tag="sAR")
        ms_sarback = sch.dma("sync",
                             lambda e: e.dma_start(out=t_qsc[0:1, 2:4], in_=d_sar_out),
                             wait=[("CC", ms_ar)], sem="SB", tag="sar_in")
        def taumath(e):
            e.tensor_scalar_mul(t_qsc[0:1, 4:6], t_qsc[0:1, 2:4],
                                cons(CONS_INV224)[0:1, :])
            e.drain()
            e.tensor_scalar(t_qsc[0:1, 12:14], t_qsc[0:1, 4:6],
                            cons(CONS_C0)[0:1, :], cons(CONS_BIAS)[0:1, :],
                            ALU.add, ALU.mult)
            e.drain()
            e.tensor_scalar_mul(t_qsc[0:1, 8:10], t_qsc[0:1, 12:14],
                                cons(CONS_P07)[0:1, :])
            e.drain()
            return e.tensor_scalar_mul(t_qsc[0:1, 10:12], t_qsc[0:1, 8:10],
                                       cons(CONS_NEG1)[0:1, :])
        ms_taumath = sch.op("vector", taumath, wait=[("SB", ms_sarback)],
                            inc=("VS", 1), tag="taumath")
        ms_taumm = sch.mm("tensor",
                          lambda e: e.matmul(big[2][:, 0:4], t_onerow[0:1, :],
                                             t_qsc[0:1, 8:12], start=True, stop=True),
                          wait=[("VS", ms_taumath)],
                          inc=("TS", 1), tag="taumm")
        ms_tau = sch.op("vector",
                        lambda e: e.tensor_copy(t_tau[:, 0:4], big[2][:, 0:4]),
                        wait=[("TS", ms_taumm)], inc=("VS", 1), tag="taup")

        t_mm_done = {}
        s_ekd = {}
        v_nd = {}
        v_done = {}
        wbf_ms = {}
        b_d2 = {}
        s_q2 = {}
        q2ld = {}
        qw = {}
        for cb in range(NCB):
            pk = big[cb % 2]
            pv = big[2 + cb % 2]
            kslot = (2 * cb) % 4
            vslot = (2 * cb + 1) % 4
            kw = [lw_idx[(cb, "k")]]
            if cb == 0:
                kw.append(("VS", ms_mix_done))
            if cb >= 2:
                kw.append(("SS", s_ekd[cb - 2]))
            for th in range(2):
                for j in range(NCB):
                    last = (th == 1 and j == NCB - 1)
                    r = sch.mm(
                        "tensor",
                        lambda e, th=th, j=j, pk=pk, kslot=kslot: e.matmul(
                            pk[:, TH * th:TH * (th + 1)],
                            wpool[:, kslot, j, :], bfA[:, j, TH * th:TH * (th + 1)],
                            start=(j == 0), stop=(j == NCB - 1),
                            skip_group_check=True),
                        wait=kw if (th == 0 and j == 0) else (),
                        inc=("TS", 1) if last else None, tag=f"kp{cb}")
            t_mm_done[(cb, "k")] = r
            vw = [lw_idx[(cb, "v")]]
            if cb in (0, 1):
                vw.append(("VS", ms_tau))
            if cb >= 2:
                vw.append(("VS", v_nd[cb - 2]))
            for th in range(2):
                for j in range(NCB):
                    last = (th == 1 and j == NCB - 1)
                    r = sch.mm(
                        "tensor",
                        lambda e, th=th, j=j, pv=pv, vslot=vslot: e.matmul(
                            pv[:, TH * th:TH * (th + 1)],
                            wpool[:, vslot, j, :], bfB[:, j, TH * th:TH * (th + 1)],
                            start=(j == 0), stop=(j == NCB - 1),
                            skip_group_check=True),
                        wait=vw if (th == 0 and j == 0) else (),
                        inc=("TS", 1) if last else None, tag=f"vp{cb}")
            t_mm_done[(cb, "v")] = r

            if cb + 2 < NCB:
                nb = cb + 2
                for which, dsrc in (("k", d_wkb), ("v", d_wvb)):
                    slot = (2 * nb + (0 if which == "k" else 1)) % 4
                    waits = [("TS", t_mm_done[(cb, which)])]
                    lw_idx[(nb, which)] = stream_blk(dsrc, nb, slot, waits,
                                                     f"w{which}{nb}")

            p = cb % 2
            ek = T6[:, p, :]
            ed = T6[:, 2 + p, :]
            sw = [("TS", t_mm_done[(cb, "k")])]
            if cb >= 2:
                sw.append(("VS", v_done[cb - 2]))
            sch.op("scalar",
                   lambda e, pk=pk, ek=ek: e.activation(ek, pk[:, :], ACT.Exp),
                   wait=sw, tag=f"ek{cb}")
            s_ekd[cb] = sch.op("scalar",
                               lambda e, pk=pk, ed=ed, cb=cb: e.activation(
                                   ed, pk[:, :], ACT.Exp, bias=t_u[:, cb:cb + 1]),
                               inc=("SS", 1), tag=f"ed{cb}")

            z = T6[:, 4, :]
            t2 = T6[:, 5, :]
            AA = T3[:, 0, :]
            BB = T3[:, 1, :]
            dbc = t_dcol[:, cb:cb + 1].broadcast_to([128, TC])
            zc = cons(CONS_ZERO)

            zw = [("SS", s_ekd[cb]), ("TS", t_mm_done[(cb, "v")])]
            if cb == 0:
                zw.append(("TS", ms_qstats))
            sch.op("vector",
                   lambda e, ek=ek, pv=pv, z=z: e.tensor_mul(z, ek, pv[:, :]),
                   wait=zw, tag=f"z{cb}")
            v_nd[cb] = sch.op("vector",
                              lambda e, ed=ed, pv=pv, t2=t2: e.tensor_mul(t2, ed, pv[:, :]),
                              inc=("VS", 1), tag=f"t2{cb}")
            sch.op("vector",
                   lambda e, dbc=dbc, z=z, AA=AA, zc=zc: e.tensor_tensor_scan(
                       AA, dbc, z, zc, ALU.mult, ALU.add),
                   tag=f"aa{cb}")

            def vnum(e, AA=AA, t2=t2, cb=cb):
                e.tensor_copy(NUM[:, cb, 0:1], t2[:, 0:1])
                e.drain()
                e.tensor_add(NUM[:, cb, 1:TC], AA[:, 0:TC - 1], t2[:, 1:TC])
                e.drain()
                return e.tensor_copy(t_sums[:, cb:cb + 1], AA[:, TC - 1:TC])
            sch.op("vector", vnum, tag=f"num{cb}")
            sch.op("vector",
                   lambda e, dbc=dbc, ek=ek, BB=BB, zc=zc: e.tensor_tensor_scan(
                       BB, dbc, ek, zc, ALU.mult, ALU.add),
                   tag=f"bb{cb}")

            def vden(e, BB=BB, ed=ed, cb=cb):
                e.tensor_copy(DEN[:, cb, 0:1], ed[:, 0:1])
                e.drain()
                e.tensor_add(DEN[:, cb, 1:TC], BB[:, 0:TC - 1], ed[:, 1:TC])
                e.drain()
                return e.tensor_copy(t_sums[:, NCB + cb:NCB + cb + 1], BB[:, TC - 1:TC])
            v_aden = sch.op("vector", vden, inc=("VS", 1), tag=f"den{cb}")
            v_done[cb] = v_aden

        ms_A_done = v_aden

        ms_sumst = sch.dma("sync",
                           lambda e: e.dma_start(out=d_sums_in, in_=t_sums[:, :]),
                           wait=[("VS", ms_A_done)], sem="SB", tag="sums_out")
        ms_sag = sch.op("gpsimd",
                        lambda e: e.collective_compute(
                            "AllGather", ALU.bypass,
                            replica_groups=[list(range(NCORES))],
                            ins=[d_sums_in], outs=[d_sums_out]),
                        wait=[("SB", ms_sumst)], inc=("CC", 1), drain=False,
                        tag="sAG")
        ms_gathld = sch.dma("sync",
                            lambda e: e.dma_start(
                                out=t_gath[:, :, :],
                                in_=d_sums_out.rearrange("(r p) c -> p r c", p=128)),
                            wait=[("CC", ms_sag)], sem="SB", tag="gath")

        def horner(e):
            for r in range(NCORES):
                e.tensor_scalar(t_dm[:, :], t_e1km1[:, :], t_maskt[:, r:r + 1],
                                cons(CONS_ONE), ALU.mult, ALU.add)
                e.drain()
                for (acc, off) in ((t_acol, 0), (t_bcol, NCB)):
                    e.tensor_mul(acc[:, :], acc[:, :], t_dm[:, :])
                    e.drain()
                    e.tensor_scalar_mul(t_hs[:, :], t_gath[:, r, off:off + NCB],
                                        t_maskt[:, r:r + 1])
                    e.drain()
                    e.tensor_add(acc[:, :], acc[:, :], t_hs[:, :])
                    e.drain()
            return e.tensor_copy(t_hs[:, 0:1], t_acol[:, 0:1])
        ms_horner = sch.op("vector", horner,
                           wait=[("SB", ms_gathld)], inc=("VS", 1), tag="horner")

        if stage == "a":
            _debug_ab(sch, stage, dbg, NUM, DEN, t_sums, t_acol, t_bcol,
                      ms_A_done, ms_A_done, ms_horner, cnt)
            stA1.close()
            stA2.close()
            _finish(nc, sch, ctx)
            return nc

        def stream_blk_sync(srcidx, slot, waits, tag):
            sem = f"LB{slot}"
            val = sch.dma(
                "scalar",
                lambda e, i=srcidx, s=slot: e.dma_start(
                    out=wpool[:, s].rearrange("p a b -> p (a b)"),
                    in_=d_wbf[i]),
                wait=waits + [("WBF", wbf_ms[srcidx])], sem=sem, tag=tag)
            return (sem, val)

        xr16 = bfA
        xr_ld = []
        for j in range(NCB):
            w = []
            if j == 0:
                w = [("XR", 16 * NCB), ("TS", t_mm_done[(NCB - 1, "k")])]
            xr_ld.append(sch.dma(
                "sync", lambda e, j=j: e.dma_start(
                    out=xr16[:, j, :], in_=d_xr[128 * j:128 * (j + 1), :]),
                wait=w, sem="XB", tag=f"xrld{j}"))

        b_mm_done = {}
        b_s_done = {}
        b_v_done = {}
        for cb in range(NCB):
            slot = cb % 4
            if cb >= 4:
                ww = [("TS", b_mm_done[cb - 4])]
            else:
                ww = [("TS", t_mm_done[(NCB - 1, "v")])]
            lwr = stream_blk(d_wrb, cb, slot, ww, f"wr{cb}")
            wbf_ms[NCB + cb] = sch.dma(
                "gpsimd", lambda e, cb=cb: e.dma_start(out=d_wbf[NCB + cb],
                                                       in_=d_wob[cb]),
                sem="WBF", tag=f"wbfo{cb}")
            pr = big[cb % 4]
            rw = [lwr, ("XB", xr_ld[-1])]
            if cb in (0, 1):
                rw.append(("VS", ms_A_done))
            if cb >= 4:
                rw.append(("SS", b_s_done[cb - 4]))
            for th in range(2):
                for j in range(NCB):
                    last = (th == 1 and j == NCB - 1)
                    r = sch.mm(
                        "tensor",
                        lambda e, th=th, j=j, pr=pr, slot=slot: e.matmul(
                            pr[:, TH * th:TH * (th + 1)],
                            wpool[:, slot, j, :], xr16[:, j, TH * th:TH * (th + 1)],
                            start=(j == 0), stop=(j == NCB - 1),
                            skip_group_check=True),
                        wait=rw if (th == 0 and j == 0) else (),
                        inc=("TS", 1) if last else None, tag=f"rp{cb}")
            b_mm_done[cb] = r

            p = cb % 2
            PW = T3[:, p, :]
            NUMf = T6[:, 3 * p + 0, :]
            DV = T6[:, 3 * p + 1, :]
            ER = T6[:, 3 * p + 2, :]

            w = [("VS", ms_horner)]
            if cb >= 2:
                w.append(("SS", b_s_done[cb - 2]))
            ms_pwv = sch.op("vector",
                            lambda e, PW=PW, cb=cb: e.tensor_scalar_mul(
                                PW, t_iota[:, :], t_weff[:, cb:cb + 1]),
                            wait=w, inc=("VS", 1), tag=f"pwv{cb}")
            ms_pws = sch.op("scalar",
                            lambda e, PW=PW: e.activation(PW, PW, ACT.Exp),
                            wait=[("VS", ms_pwv)], inc=("SS", 1), tag=f"pws{cb}")
            erw = [("TS", b_mm_done[cb])]
            if cb >= 2:
                erw.append(("VS", b_d2[cb - 2]))
            ms_er = sch.op("scalar",
                           lambda e, pr=pr, ER=ER: e.activation(
                               ER, pr[:, :], ACT.Exp, scale=cons(CONS_NEG1)),
                           wait=erw, inc=("SS", 1), tag=f"er{cb}")

            sch.op("vector",
                   lambda e, PW=PW, NUMf=NUMf, cb=cb: e.scalar_tensor_tensor(
                       NUMf, PW, t_acol[:, cb:cb + 1], NUM[:, cb, :],
                       ALU.mult, ALU.add),
                   wait=[("SS", ms_pws)], tag=f"numf{cb}")
            sch.op("vector",
                   lambda e, PW=PW, DV=DV, cb=cb: e.scalar_tensor_tensor(
                       DV, PW, t_bcol[:, cb:cb + 1], DEN[:, cb, :],
                       ALU.mult, ALU.add),
                   tag=f"denf{cb}")
            sch.op("vector",
                   lambda e, ER=ER: e.tensor_scalar(ER, ER, cons(CONS_ONE), None,
                                                    ALU.add),
                   wait=[("SS", ms_er)], tag=f"den2{cb}")
            b_d2[cb] = sch.op("vector",
                              lambda e, DV=DV, ER=ER: e.tensor_mul(DV, DV, ER),
                              inc=("VS", 1), tag=f"d2{cb}")
            sch.op("scalar",
                   lambda e, DV=DV, ER=ER: e.activation(ER, DV, ACT.Ln),
                   wait=[("VS", b_d2[cb])], tag=f"ln{cb}")
            b_s_done[cb] = sch.op("scalar",
                                  lambda e, ER=ER, DV=DV: e.activation(
                                      DV, ER, ACT.Exp, scale=cons(CONS_NEG1)),
                                  inc=("SS", 1), tag=f"rinv{cb}")
            b_v_done[cb] = sch.op("vector",
                                  lambda e, NUMf=NUMf, DV=DV, cb=cb: e.tensor_mul(
                                      NUM[:, cb, :], NUMf, DV),
                                  wait=[("SS", b_s_done[cb])], inc=("VS", 1),
                                  tag=f"s{cb}")

            for i in (2 * cb, 2 * cb + 1):
                dqin = d_q1in if i < 16 else d_q2in
                bi = (i % 16) // 2
                hf = (i % 16) % 2
                tcol = 0 if i < 16 else 1
                qw2w = [("VS", ms_tau)]
                if i == 0:
                    qw2w.append(("SS", gp_red[-1]))
                else:
                    qw2w.append(("VS", s_q2[i - 1]))
                q2ld[i] = sch.dma(
                    "sync",
                    lambda e, s=qsrc[i]: e.dma_start(out=qring[:, :], in_=s),
                    wait=qw2w, sem="LQ2", tag=f"q2ld{i}")

                def qtern(e, i=i, tcol=tcol):
                    e.tensor_single_scalar(qout[:, :], qring[:, :],
                                           t_tau[:, 2 + tcol:3 + tcol], ALU.is_le)
                    e.drain()
                    return e.scalar_tensor_tensor(qp2[:, :], qring[:, :],
                                                  t_tau[:, tcol:tcol + 1],
                                                  qout[:, :], ALU.is_ge,
                                                  ALU.subtract)
                qtw = [("LQ2", q2ld[i])]
                if i > 0:
                    qtw.append(("QW", qw[i - 1]))
                s_q2[i] = sch.op("vector", qtern, wait=qtw, inc=("VS", 1),
                                 tag=f"qtern{i}")
                qw[i] = sch.dma(
                    "sync",
                    lambda e, d=dqin, b=bi, hf=hf: e.dma_start(
                        out=d[b][:, 1024 * hf:1024 * (hf + 1)],
                        in_=qp2[:, :]),
                    wait=[("VS", s_q2[i])], sem="QW", tag=f"qw_{i}")




        ms_B_done = b_v_done[NCB - 1]

        ms_ag1 = sch.op("gpsimd",
                        lambda e: e.collective_compute(
                            "AllGather", ALU.bypass,
                            replica_groups=[list(range(NCORES))],
                            ins=[d_q1in], outs=[d_q1all]),
                        wait=[("QW", cnt["QW"])], inc=("CC", 1), drain=False,
                        tag="qAG1")
        qb1_ms = {}
        qb2_ms = {}
        for fc in range(NFB):
            w = [("CC", ms_ag1)] if fc == 0 else []
            qb1_ms[fc] = sch.dma(
                "gpsimd", lambda e, fc=fc: e.dma_start(out=d_q1bf[fc],
                                                       in_=d_q1all[fc]),
                wait=w, sem="QB1", tag=f"qb1_{fc}")
        ms_ag2 = sch.op("gpsimd",
                        lambda e: e.collective_compute(
                            "AllGather", ALU.bypass,
                            replica_groups=[list(range(NCORES))],
                            ins=[d_q2in], outs=[d_q2all]),
                        inc=("CC", 1), drain=False, tag="qAG2")
        for gi in range(NFB):
            w = [("CC", ms_ag2)] if gi == 0 else []
            qb2_ms[gi] = sch.dma(
                "gpsimd", lambda e, gi=gi: e.dma_start(out=d_q2bf[gi],
                                                       in_=d_q2all[gi]),
                wait=w, sem="QB2", tag=f"qb2_{gi}")


        if stage == "b":
            _debug_ab(sch, stage, dbg, NUM, DEN, t_sums, t_acol, t_bcol,
                      ms_A_done, ms_B_done, ms_horner, cnt)
            stA1.close()
            stA2.close()
            _finish(nc, sch, ctx)
            return nc

        stA1.close()
        stC = ExitStack()
        sqb = sb("sqb", [128, TC], BF16, st=stC)
        row_rstd = xin[0:1, 0, :]
        row_mr = xin[0:1, 1, :]

        S = NUM
        c_mm_done = {}
        c_v_done = {}
        c_st_done = {}
        x2_ld = []
        for cb in range(NCB):
            slot = cb % 4
            ww = [("TS", c_mm_done[cb - 4] if cb >= 4 else b_mm_done[NCB - 1])]
            lwo = stream_blk_sync(NCB + cb, slot, ww, f"wo{cb}")
            w = [("VS", c_v_done[cb - 2])] if cb >= 2 else []
            x2_ld.append(sch.dma(
                "sync", lambda e, cb=cb: e.dma_start(
                    out=xin[:, cb % 2, :], in_=d_x[128 * cb:128 * (cb + 1), :]),
                wait=w, sem=f"XC{cb % 2}", tag=f"x2ld{cb}"))
            po = big[cb % 2]
            mw = [lwo, ("VS", ms_B_done)]
            if cb >= 2:
                mw.append(("VS", c_v_done[cb - 2]))
            for th in range(2):
                for j in range(NCB):
                    last = (th == 1 and j == NCB - 1)
                    r = sch.mm(
                        "tensor",
                        lambda e, th=th, j=j, po=po, slot=slot: e.matmul(
                            po[:, TH * th:TH * (th + 1)],
                            wpool[:, slot, j, :], S[:, j, TH * th:TH * (th + 1)],
                            start=(j == 0), stop=(j == NCB - 1),
                            skip_group_check=True),
                        wait=mw if (th == 0 and j == 0) else (),
                        inc=("TS", 1) if last else None, tag=f"wo{cb}")
            c_mm_done[cb] = r

            def vres(e, po=po, cb=cb):
                e.tensor_add(bfA[:, cb, :], po[:, :], xin[:, cb % 2, :])
                e.drain()
                return e.tensor_mul(sqb[:, :], bfA[:, cb, :], bfA[:, cb, :])
            vw = [("TS", c_mm_done[cb]), (f"XC{cb % 2}", x2_ld[cb])]
            if cb >= 1:
                vw.append(("TS", c_st_done[cb - 1]))
            c_v_done[cb] = sch.op("vector", vres, wait=vw, inc=("VS", 1),
                                  tag=f"res{cb}")
            for th in range(2):
                sch.mm("tensor",
                       lambda e, th=th, cb=cb: e.matmul(
                           big[2][0:1, TH * th:TH * (th + 1)], t_onebf[:, 0:1],
                           bfA[:, cb, TH * th:TH * (th + 1)],
                           start=(cb == 0), stop=(cb == NCB - 1),
                           skip_group_check=True),
                       wait=[("VS", c_v_done[cb])] if th == 0 else (),
                       tag=f"st1_{cb}")
                c_st_done[cb] = sch.mm(
                    "tensor",
                    lambda e, th=th, cb=cb: e.matmul(
                        big[3][0:1, TH * th:TH * (th + 1)], t_onebf[:, 0:1],
                        sqb[:, TH * th:TH * (th + 1)],
                        start=(cb == 0), stop=(cb == NCB - 1),
                        skip_group_check=True),
                    inc=("TS", 1) if th == 1 else None, tag=f"st2_{cb}")
        ms_stats1 = c_st_done[NCB - 1]

        def ln_rows(e):
            e.tensor_scalar_mul(row_rstd, big[2][0:1, :],
                                cons(CONS_INV2048)[0:1, :])
            e.drain()
            e.tensor_scalar_mul(row_mr, big[3][0:1, :],
                                cons(CONS_INV2048)[0:1, :])
            e.drain()
            e.tensor_mul(big[2][32:33, :], row_rstd, row_rstd)
            e.drain()
            e.tensor_sub(big[3][32:33, :], row_mr, big[2][32:33, :])
            e.drain()
            return e.tensor_scalar(big[3][64:65, :], big[3][32:33, :],
                                   cons(CONS_EPS)[0:1, :], None, ALU.add)
        ms_lnr = sch.op("vector", ln_rows, wait=[("TS", ms_stats1)],
                        inc=("VS", 1), tag="ln1rows")

        def ln_rstd(e):
            e.activation(big[2][64:65, :], big[3][64:65, :], ACT.Ln)
            e.drain()
            return e.activation(big[2][32:33, :], big[2][64:65, :], ACT.Exp,
                                scale=cons(CONS_NHALF)[0:1, :])
        ms_rstd = sch.op("scalar", ln_rstd, wait=[("VS", ms_lnr)],
                         inc=("SS", 1), tag="ln1rstd")

        def ln_mr(e):
            e.tensor_mul(row_mr, row_rstd, big[2][32:33, :])
            e.drain()
            return e.tensor_copy(row_rstd, big[2][32:33, :])
        ms_mr = sch.op("vector", ln_mr,
                       wait=[("SS", ms_rstd)], inc=("VS", 1), tag="ln1mr")
        bc_ms = None
        for th in range(2):
            sch.mm("tensor",
                   lambda e, th=th: e.matmul(
                       big[2][:, TH * th:TH * (th + 1)], t_onerow[0:1, :],
                       row_rstd[:, TH * th:TH * (th + 1)], start=True, stop=True),
                   wait=[("VS", ms_mr)] if th == 0 else (), tag="bc1")
            bc_ms = sch.mm("tensor",
                           lambda e, th=th: e.matmul(
                               big[3][:, TH * th:TH * (th + 1)], t_onerow[0:1, :],
                               row_mr[:, TH * th:TH * (th + 1)],
                               start=True, stop=True),
                           inc=("TS", 1) if th == 1 else None, tag="bc2")

        ap_ms = {}
        for cb in range(NCB):
            def vap(e, cb=cb):
                e.tensor_mul(sqb[:, :], bfA[:, cb, :], big[2][:, :])
                e.drain()
                return e.tensor_sub(bfB[:, cb, :], sqb[:, :], big[3][:, :])
            ap_ms[cb] = sch.op("vector", vap,
                               wait=[("TS", bc_ms)] if cb == 0 else (),
                               inc=("VS", 1), tag=f"ap1_{cb}")
        ms_C_done = ap_ms[NCB - 1]

        if stage == "c":
            _debug_c(sch, dbg, bfA, bfB, ms_C_done, cnt)
            stC.close()
            stA2.close()
            _finish(nc, sch, ctx)
            return nc

        stC.close()
        stA2.close()
        stF = ctx.enter_context(ExitStack())
        hbuf = sb("hbuf", [128, NFB, TH], BF16, st=stF)
        out2 = sb("out2", [128, NCB, TH], F32, st=stF)
        ln2s = sb("ln2s", [128, TH], F32, st=stF)
        sq2 = sb("sq2", [128, TH], BF16, st=stF)
        wpx = sb("wpx", [128, 4, NCB, 128], BF16, st=stF)

        def fslot(s):
            return wpool[:, s, :, :] if s < 4 else wpx[:, s - 4, :, :]

        prev_f2_v = None
        prev_sh_last = None
        prev_f2_ts = None
        out_st = []
        fslot_user = {0: c_mm_done[NCB - 4], 1: c_mm_done[NCB - 3],
                      2: c_mm_done[NCB - 2], 3: c_mm_done[NCB - 1]}
        for th in range(2):
            f1_done = {}
            s_h = {}
            for fc in range(NFB):
                slot = fc % 8
                ww = [("QB1", qb1_ms[fc])]
                if slot in fslot_user:
                    ww.append(("TS", fslot_user[slot]))
                lf = (f"LF{slot}", sch.dma("sync",
                             lambda e, fc=fc, slot=slot: e.dma_start(
                                 out=fslot(slot).rearrange("p a b -> p (a b)"),
                                 in_=d_q1bf[fc]),
                             wait=ww, sem=f"LF{slot}", tag=f"f1w{th}_{fc}"))
                ph = big[0][:, TH * (fc % 2):TH * (fc % 2 + 1)]
                mw = [lf]
                if th == 0 and fc == 0:
                    mw.append(("VS", ms_C_done))
                if prev_sh_last is not None and fc == 0:
                    mw.append(("SS", prev_sh_last))
                if fc >= 2:
                    mw.append(("SS", s_h[fc - 2]))
                for j in range(NCB):
                    r = sch.mm(
                        "tensor",
                        lambda e, j=j, ph=ph, slot=slot, th=th: e.matmul(
                            ph, fslot(slot)[:, j, :],
                            bfB[:, j, TH * th:TH * (th + 1)],
                            start=(j == 0), stop=(j == NCB - 1),
                            skip_group_check=True),
                        wait=mw if j == 0 else (),
                        inc=("TS", 1) if j == NCB - 1 else None,
                        tag=f"f1_{th}_{fc}")
                f1_done[fc] = r
                fslot_user[slot] = r
                s_h[fc] = sch.op(
                    "scalar",
                    lambda e, ph=ph, fc=fc: e.activation(
                        hbuf[:, fc, :], ph, ACT.Gelu, bias=t_b1[:, fc:fc + 1]),
                    wait=[("TS", f1_done[fc])], inc=("SS", 1), tag=f"gelu{th}_{fc}")

            f2_done = {}
            f2v = {}
            for dc in range(NCB):
                qslots = [4 * (dc % 2) + q for q in range(4)]
                lf2s = []
                for q in range(4):
                    slot = qslots[q]
                    gi = 8 * (dc // 2) + 4 * (dc % 2) + q
                    ww = [("QB2", qb2_ms[gi])]
                    if slot in fslot_user:
                        ww.append(("TS", fslot_user[slot]))
                    lf2s.append((f"LF{slot}", sch.dma("scalar",
                                  lambda e, gi=gi, slot=slot: e.dma_start(
                                      out=fslot(slot).rearrange("p a b -> p (a b)"),
                                      in_=d_q2bf[gi]),
                                  wait=ww, sem=f"LF{slot}", tag=f"f2w{th}_{dc}_{q}")))
                po2 = big[1][:, TH * (dc % 2):TH * (dc % 2 + 1)]
                mw = lf2s + [("SS", s_h[NFB - 1])]
                if dc >= 2:
                    mw.append(("VS", f2v[dc - 2]))
                for q in range(4):
                    slot = qslots[q]
                    for j in range(NCB):
                        fj = 16 * q + j
                        r = sch.mm(
                            "tensor",
                            lambda e, q=q, j=j, po2=po2, slot=slot, fj=fj: e.matmul(
                                po2, fslot(slot)[:, j, :], hbuf[:, fj, :],
                                start=(fj == 0), stop=(fj == NFB - 1),
                                skip_group_check=True),
                            wait=mw if (q == 0 and j == 0) else (),
                            inc=("TS", 1) if j == NCB - 1 else None,
                            tag=f"f2_{th}_{dc}_{q}")
                    fslot_user[slot] = r
                f2_done[dc] = r

                def vevac(e, po2=po2, dc=dc, th=th):
                    e.tensor_scalar(out2[:, dc, :], po2, t_b2[:, dc:dc + 1],
                                    None, ALU.add)
                    e.drain()
                    e.tensor_add(out2[:, dc, :], out2[:, dc, :],
                                 bfB[:, dc, TH * th:TH * (th + 1)])
                    e.drain()
                    return e.tensor_mul(sq2[:, :], out2[:, dc, :], out2[:, dc, :])
                vw = [("TS", f2_done[dc])]
                if dc >= 1:
                    vw.append(("TS", c_st_done[("f2", th, dc - 1)]))
                if th == 1 and dc == 0:
                    vw.append(("SO", out_st[-1]))
                f2v[dc] = sch.op("vector", vevac, wait=vw, inc=("VS", 1),
                                 tag=f"f2v{th}_{dc}")
                sch.mm("tensor",
                       lambda e, dc=dc: e.matmul(
                           big[2][0:1, 0:TH], t_onef[:, 0:1], out2[:, dc, :],
                           start=(dc == 0), stop=(dc == NCB - 1),
                           skip_group_check=True),
                       wait=[("VS", f2v[dc])], tag=f"st3_{th}_{dc}")
                c_st_done[("f2", th, dc)] = sch.mm(
                    "tensor",
                    lambda e, dc=dc: e.matmul(
                        big[3][0:1, 0:TH], t_onebf[:, 0:1], sq2[:, :],
                        start=(dc == 0), stop=(dc == NCB - 1),
                        skip_group_check=True),
                    inc=("TS", 1), tag=f"st4_{th}_{dc}")
            ms_stats2 = c_st_done[("f2", th, NCB - 1)]

            def ln2_rows(e):
                e.tensor_scalar_mul(row_rstd[:, 0:TH], big[2][0:1, 0:TH],
                                    cons(CONS_INV2048)[0:1, :])
                e.drain()
                e.tensor_scalar_mul(row_mr[:, 0:TH], big[3][0:1, 0:TH],
                                    cons(CONS_INV2048)[0:1, :])
                e.drain()
                e.tensor_mul(big[2][32:33, 0:TH], row_rstd[:, 0:TH],
                             row_rstd[:, 0:TH])
                e.drain()
                e.tensor_sub(big[3][32:33, 0:TH], row_mr[:, 0:TH],
                             big[2][32:33, 0:TH])
                e.drain()
                return e.tensor_scalar(big[3][64:65, 0:TH], big[3][32:33, 0:TH],
                                       cons(CONS_EPS)[0:1, :], None, ALU.add)
            ms_lnr2 = sch.op("vector", ln2_rows, wait=[("TS", ms_stats2)],
                             inc=("VS", 1), tag=f"ln2rows{th}")

            def ln2_rstd(e):
                e.activation(big[2][64:65, 0:TH], big[3][64:65, 0:TH], ACT.Ln)
                e.drain()
                return e.activation(big[2][32:33, 0:TH], big[2][64:65, 0:TH],
                                    ACT.Exp, scale=cons(CONS_NHALF)[0:1, :])
            ms_rstd2 = sch.op("scalar", ln2_rstd, wait=[("VS", ms_lnr2)],
                              inc=("SS", 1), tag=f"ln2rstd{th}")

            def ln2_mr(e):
                e.tensor_mul(row_mr[:, 0:TH], row_rstd[:, 0:TH],
                             big[2][32:33, 0:TH])
                e.drain()
                return e.tensor_copy(row_rstd[:, 0:TH], big[2][32:33, 0:TH])
            ms_mr2 = sch.op("vector", ln2_mr,
                            wait=[("SS", ms_rstd2)], inc=("VS", 1), tag=f"ln2mr{th}")
            sch.mm("tensor",
                   lambda e: e.matmul(big[2][:, 0:TH], t_onerow[0:1, :],
                                      row_rstd[:, 0:TH], start=True, stop=True),
                   wait=[("VS", ms_mr2)], tag=f"bc3_{th}")
            bc2_ms = sch.mm("tensor",
                            lambda e: e.matmul(big[3][:, 0:TH], t_onerow[0:1, :],
                                               row_mr[:, 0:TH],
                                               start=True, stop=True),
                            inc=("TS", 1), tag=f"bc4_{th}")
            last_ap = None
            for dc in range(NCB):
                def vap2(e, dc=dc):
                    e.tensor_mul(ln2s[:, :], out2[:, dc, :], big[2][:, 0:TH])
                    e.drain()
                    return e.tensor_sub(out2[:, dc, :], ln2s[:, :], big[3][:, 0:TH])
                last_ap = sch.op("vector", vap2,
                                 wait=[("TS", bc2_ms)] if dc == 0 else (),
                                 inc=("VS", 1), tag=f"ap2_{th}_{dc}")
                out_st.append(sch.dma(
                    "sync",
                    lambda e, dc=dc, th=th: e.dma_start(
                        out=d_out[128 * dc:128 * (dc + 1), TH * th:TH * (th + 1)],
                        in_=out2[:, dc, :]),
                    wait=[("VS", last_ap)], sem="SO", tag=f"ost{th}_{dc}"))
            prev_f2_v = last_ap
            prev_sh_last = s_h[NFB - 1]
            prev_f2_ts = f2_done[NCB - 1]

        if stage == "cf":
            dbgq1 = nc.dram_tensor("dbgq1", [NFB, C, 128], BF16, kind="ExternalOutput").ap()
            dbgq2 = nc.dram_tensor("dbgq2", [NFB, C, 128], BF16, kind="ExternalOutput").ap()
            sch.dma("gpsimd", lambda e: e.dma_start(out=dbgq1, in_=d_q1all),
                    wait=[("SO", out_st[-1])], sem="DB", tag="dbgq1")
            sch.dma("gpsimd", lambda e: e.dma_start(out=dbgq2, in_=d_q2all),
                    sem="DB", tag="dbgq2")
            dbgh = nc.dram_tensor("dbgh", [FF, TH], F32, kind="ExternalOutput").ap()
            for fc in range(NFB):
                sch.dma("gpsimd", lambda e, fc=fc: e.dma_start(
                    out=dbgh[128 * fc:128 * (fc + 1), :], in_=hbuf[:, fc, :]),
                    wait=[("SO", out_st[-1])], sem="DB", tag=f"dbgh{fc}")
            for j in range(NCB):
                sch.dma("gpsimd", lambda e, j=j: e.dma_start(
                    out=dbg[0][128 * j:128 * (j + 1), :], in_=bfA[:, j, :]),
                    wait=[("SO", out_st[-1])], sem="DB", tag=f"dbg0_{j}")
                sch.dma("gpsimd", lambda e, j=j: e.dma_start(
                    out=dbg[1][128 * j:128 * (j + 1), :], in_=bfB[:, j, :]),
                    sem="DB", tag=f"dbg1_{j}")
            sch.op("gpsimd", lambda e: e.nop(), wait=[("DB", cnt["DB"])],
                   drain=False, tag="dbg_wait")

        sch.op("sync", lambda e: e.nop(), wait=[("SO", out_st[-1])],
               drain=False, tag="final")

        _finish(nc, sch, ctx)
    return nc


def _finish(nc, sch, ctx):
    sems = {}
    names = sorted({s for ops in sch.ops.values() for (w, _, i, _, _) in ops
                    for s in ([x[0] for x in w] + ([i[0]] if i else []))})
    for n in names:
        sems[n] = ctx.enter_context(nc.semaphore(n))
    block = ctx.enter_context(nc.Block())

    def emit(engname):
        def run(eng):
            for wait, fn, inc, drain, tag in sch.ops[engname]:
                for s, v in wait:
                    eng.wait_ge(sems[s], v)
                inst = fn(eng)
                if drain:
                    d = eng.drain()
                    if inc is not None:
                        d.then_inc(sems[inc[0]], inc[1])
                elif inc is not None:
                    inst.then_inc(sems[inc[0]], inc[1])
        return run

    block.sync(emit("sync"))
    block.gpsimd(emit("gpsimd"))
    block.tensor(emit("tensor"))
    block.vector(emit("vector"))
    block.scalar(emit("scalar"))


def _debug_ab(sch, stage, dbg, NUM, DEN, t_sums, t_acol, t_bcol,
              ms_A_done, ms_B_done, ms_horner, cnt):
    ms = ms_A_done if stage == "a" else ms_B_done
    for j in range(NCB):
        sch.dma("gpsimd", lambda e, j=j: e.dma_start(
            out=dbg[0][128 * j:128 * (j + 1), :], in_=NUM[:, j, :]),
            wait=[("VS", ms)], sem="DB", tag=f"dbg0_{j}")
        sch.dma("gpsimd", lambda e, j=j: e.dma_start(
            out=dbg[1][128 * j:128 * (j + 1), :], in_=DEN[:, j, :]),
            sem="DB", tag=f"dbg1_{j}")
    sch.dma("gpsimd", lambda e: e.dma_start(out=dbg["s"][:, 0:32], in_=t_sums[:, :]),
            sem="DB", tag="dbgs")
    if stage == "b":
        sch.dma("gpsimd", lambda e: e.dma_start(out=dbg["s"][:, 32:48],
                                                in_=t_acol[:, :]),
                wait=[("VS", ms_horner)], sem="DB", tag="dbga")
        sch.dma("gpsimd", lambda e: e.dma_start(out=dbg["s"][:, 48:64],
                                                in_=t_bcol[:, :]),
                sem="DB", tag="dbgb")
    sch.op("gpsimd", lambda e: e.nop(), wait=[("DB", cnt["DB"])],
           drain=False, tag="dbg_wait")


def _debug_c(sch, dbg, bfA, bfB, ms_C_done, cnt):
    for j in range(NCB):
        sch.dma("gpsimd", lambda e, j=j: e.dma_start(
            out=dbg[0][128 * j:128 * (j + 1), :], in_=bfA[:, j, :]),
            wait=[("VS", ms_C_done)], sem="DB", tag=f"dbg0_{j}")
        sch.dma("gpsimd", lambda e, j=j: e.dma_start(
            out=dbg[1][128 * j:128 * (j + 1), :], in_=bfB[:, j, :]),
            sem="DB", tag=f"dbg1_{j}")
    sch.op("gpsimd", lambda e: e.nop(), wait=[("DB", cnt["DB"])],
           drain=False, tag="dbg_wait")



def _blocks(wT, nb):
    K, M = wT.shape
    assert M == nb * 128 and K % 128 == 0
    nj = K // 128
    b = wT.reshape(nj, 128, nb, 128)
    b = b.transpose(2, 1, 0, 3)
    return np.ascontiguousarray(b.reshape(nb, 128, K))


def _cols(v):
    return np.ascontiguousarray(np.asarray(v, dtype=np.float32).reshape(-1, 128).T)


def make_in_maps(inputs):
    x = np.ascontiguousarray(np.asarray(inputs["L0_attn_x"], dtype=np.float32)[0])
    Wr = np.asarray(inputs["L0_attn_Wr"], dtype=np.float32)
    Wk = np.asarray(inputs["L0_attn_Wk"], dtype=np.float32)
    Wv = np.asarray(inputs["L0_attn_Wv"], dtype=np.float32)
    Wo = np.asarray(inputs["L0_attn_Wo"], dtype=np.float32)
    w1 = np.asarray(inputs["L0_ffn_w1"], dtype=np.float32)
    w2 = np.asarray(inputs["L0_ffn_w2"], dtype=np.float32)

    wkb = _blocks(np.ascontiguousarray(Wk.T), NCB)
    wvb = _blocks(np.ascontiguousarray(Wv.T), NCB)
    wrb = _blocks(np.ascontiguousarray(Wr.T), NCB)
    wob = _blocks(np.ascontiguousarray(Wo.T), NCB)
    w1b = _blocks(np.ascontiguousarray(w1.T), NFB)
    w2tb = np.ascontiguousarray(w2.T).reshape(FF, NCB, 128)
    w2q = np.empty((NCB, 4, 128, 2048), dtype=np.float32)
    for dc in range(NCB):
        blk = w2tb[:, dc, :]
        q = blk.reshape(4, 16, 128, 128)
        w2q[dc] = q.transpose(0, 2, 1, 3).reshape(4, 128, 2048)

    muk = _cols(inputs["L0_attn_mu_k"])
    mur = _cols(inputs["L0_attn_mu_r"])
    muv = _cols(inputs["L0_attn_mu_v"])
    uv = _cols(inputs["L0_attn_wkv_u"])
    wv_ = _cols(inputs["L0_attn_wkv_w"])
    b1 = _cols(inputs["L0_ffn_b1"])
    b2 = _cols(inputs["L0_ffn_b2"])

    in_maps = []
    for c in range(NCORES):
        sl = slice(TC * c, TC * (c + 1))
        xc = np.ascontiguousarray(x[sl].T)
        if c == 0:
            xp = np.zeros((128, NCB), dtype=np.float32)
        else:
            xp = _cols(x[TC * c - 1])
        mask = np.zeros(NCORES, dtype=np.float32)
        mask[:c] = 1.0
        maskb = np.ascontiguousarray(np.broadcast_to(mask, (128, NCORES)))
        w1s = np.ascontiguousarray(w1b[8 * c:8 * (c + 1)])
        w2s = np.ascontiguousarray(w2q[2 * c:2 * (c + 1)].reshape(8, 128, 2048))
        in_maps.append({
            "x": xc, "xprev": xp, "wkb": wkb, "wvb": wvb, "wrb": wrb,
            "wob": wob, "w1s": w1s, "w2s": w2s, "muk": muk, "mur": mur,
            "muv": muv, "uv": uv, "wv": wv_, "b1": b1, "b2": b2,
            "mask": maskb,
        })
    return in_maps


def kernel(**inputs):
    in_maps = make_in_maps(inputs)
    nc = build_program("full")
    res = run_bass_kernel_spmd(nc, in_maps, core_ids=list(range(NCORES)))
    out = np.empty((NCORES * TC, C), dtype=np.float32)
    for c in range(NCORES):
        out[TC * c:TC * (c + 1), :] = res.results[c]["out"].T
    return out.reshape(1, NCORES * TC, C)


# version 17
# speedup vs baseline: 1.1111x; 1.1111x over previous
import numpy as np
import ml_dtypes
from collections import defaultdict
from contextlib import ExitStack

import concourse.bass as bass
import concourse.mybir as mybir
from concourse.bass_utils import run_bass_kernel_spmd

F32 = mybir.dt.float32
BF16 = mybir.dt.bfloat16
ALU = mybir.AluOpType
ACT = mybir.ActivationFunctionType

NCORES = 8
TC = 1024
C = 2048
NCB = 16
TH = 512
FF = 8192
NFB = 64
NQB = 16
LN_EPS = 1e-5

ENGINES = ("sync", "gpsimd", "tensor", "vector", "scalar")


class Sched:

    def __init__(self):
        self.ops = {e: [] for e in ENGINES}
        self.cnt = defaultdict(int)

    def op(self, eng, fn, wait=(), inc=None, drain=True, tag=""):
        for s, v in wait:
            assert isinstance(v, int), f"wait {s} is {v!r} at {tag}"
        after = None
        if inc is not None:
            s, a = inc
            self.cnt[s] += a
            after = self.cnt[s]
        self.ops[eng].append((tuple(wait), fn, inc, drain, tag))
        return after

    def dma(self, eng, fn, wait=(), sem="LD", tag=""):
        return self.op(eng, fn, wait=wait, inc=(sem, 16), drain=False, tag=tag)

    def mm(self, eng, fn, wait=(), inc=None, tag=""):
        return self.op(eng, fn, wait=wait, inc=inc, drain=False, tag=tag)


def build_program(stage="full"):
    nc = bass.Bass("TRN2", target_bir_lowering=False)

    d_x = nc.dram_tensor("x", [C, TC], F32, kind="ExternalInput").ap()
    d_xprev = nc.dram_tensor("xprev", [128, NCB], F32, kind="ExternalInput").ap()
    d_wkb = nc.dram_tensor("wkb", [NCB, 128, C], F32, kind="ExternalInput").ap()
    d_wvb = nc.dram_tensor("wvb", [NCB, 128, C], F32, kind="ExternalInput").ap()
    d_wrb = nc.dram_tensor("wrb", [NCB, 128, C], F32, kind="ExternalInput").ap()
    d_wob = nc.dram_tensor("wob", [NCB, 128, C], F32, kind="ExternalInput").ap()
    d_w1s = nc.dram_tensor("w1s", [8, 128, C], F32, kind="ExternalInput").ap()
    d_w2s = nc.dram_tensor("w2s", [8, 128, C], F32, kind="ExternalInput").ap()
    d_muk = nc.dram_tensor("muk", [128, NCB], F32, kind="ExternalInput").ap()
    d_mur = nc.dram_tensor("mur", [128, NCB], F32, kind="ExternalInput").ap()
    d_muv = nc.dram_tensor("muv", [128, NCB], F32, kind="ExternalInput").ap()
    d_uv = nc.dram_tensor("uv", [128, NCB], F32, kind="ExternalInput").ap()
    d_wv = nc.dram_tensor("wv", [128, NCB], F32, kind="ExternalInput").ap()
    d_b1 = nc.dram_tensor("b1", [128, NFB], F32, kind="ExternalInput").ap()
    d_b2 = nc.dram_tensor("b2", [128, NCB], F32, kind="ExternalInput").ap()
    d_mask = nc.dram_tensor("mask", [128, NCORES], F32, kind="ExternalInput").ap()
    d_out = nc.dram_tensor("out", [C, TC], F32, kind="ExternalOutput").ap()

    d_xr = nc.dram_tensor("xr_scr", [C, TC], BF16).ap()
    d_q1in = nc.dram_tensor("q1in", [8, 128, C], mybir.dt.int8).ap()
    d_q2in = nc.dram_tensor("q2in", [8, 128, C], mybir.dt.int8).ap()
    d_q1all = nc.dram_tensor("q1all", [NFB, 128, C], mybir.dt.int8).ap()
    d_q2all = nc.dram_tensor("q2all", [NFB, 128, C], mybir.dt.int8).ap()
    d_sar_in = nc.dram_tensor("sar_in", [1, 2], F32).ap()
    d_sar_out = nc.dram_tensor("sar_out", [1, 2], F32).ap()
    d_wbf = nc.dram_tensor("wbf", [2 * NCB, 128, C], BF16).ap()
    d_sums_in = nc.dram_tensor("sums_in", [128, 32], F32).ap()
    d_sums_out = nc.dram_tensor("sums_out", [128 * NCORES, 32], F32).ap()

    dbg = {}
    if stage not in ("full",):
        for i in range(2):
            dbg[i] = nc.dram_tensor(f"dbg{i}", [C, TC], F32, kind="ExternalOutput").ap()
        dbg["s"] = nc.dram_tensor("dbgs", [128, 64], F32, kind="ExternalOutput").ap()

    sch = Sched()
    cnt = sch.cnt

    with ExitStack() as ctx:
        def sb(name, shape, dt=F32, st=None):
            return (st or ctx).enter_context(nc.sbuf_tensor(name, shape, dt))

        def psum(name, shape):
            return ctx.enter_context(nc.psum_tensor(name, shape, F32))

        t_muk = sb("t_muk", [128, NCB])
        t_amk = sb("t_amk", [128, NCB])
        t_amr = sb("t_amr", [128, NCB])
        t_amv = sb("t_amv", [128, NCB])
        t_xprev = sb("t_xprev", [128, NCB])
        t_u = sb("t_u", [128, NCB])
        t_w = sb("t_w", [128, NCB])
        t_weff = sb("t_weff", [128, NCB])
        t_dcol = sb("t_dcol", [128, NCB])
        t_e1k = sb("t_e1k", [128, NCB])
        t_e1km1 = sb("t_e1km1", [128, NCB])
        t_acol = sb("t_acol", [128, NCB])
        t_bcol = sb("t_bcol", [128, NCB])
        t_dm = sb("t_dm", [128, NCB])
        t_hs = sb("t_hs", [128, NCB])
        t_sums = sb("t_sums", [128, 32])
        t_gath = sb("t_gath", [128, NCORES, 32])
        t_maskt = sb("t_maskt", [128, NCORES])
        t_b1 = sb("t_b1", [128, NFB])
        t_b2 = sb("t_b2", [128, NCB])
        t_cons = sb("t_cons", [128, 16])
        t_onebf = sb("t_onebf", [128, 1], BF16)
        t_onef = sb("t_onef", [128, 1], F32)
        t_onerow = sb("t_onerow", [1, 128], F32)
        t_qsc = sb("t_qsc", [1, 48])
        t_tau = sb("t_tau", [128, 4])
        bfA = sb("bfA", [128, NCB, TC], BF16)
        bfB = sb("bfB", [128, NCB, TC], BF16)
        wpool = sb("wpool", [128, 4, NCB, 128], BF16)
        xin = sb("xin", [128, 2, TC], F32)

        (CONS_ZERO, CONS_ONE, CONS_NEG1, CONS_HALF, CONS_NHALF, CONS_INV2048,
         CONS_EPS, CONS_NQF, CONS_NC0, CONS_C0, CONS_INV224, CONS_P07,
         CONS_BIAS) = range(13)

        def cons(i):
            return t_cons[:, i:i + 1]

        big = [psum(f"big{i}", [128, TC]) for i in range(4)]

        stA2 = ExitStack()
        NUM = sb("NUM", [128, NCB, TC], BF16, st=stA2)
        DEN = sb("DEN", [128, NCB, TC], BF16, st=stA2)
        stA1 = ExitStack()
        T6 = sb("T6", [128, 6, TC], F32, st=stA1)
        T3 = sb("T3", [128, 2, TC], F32, st=stA1)
        t_iota = sb("t_iota", [128, TC], F32, st=stA1)
        xrring = sb("xrring", [128, 1, TC], BF16, st=stA1)
        qring = sb("qring", [128, 1024], F32, st=stA1)
        qout = sb("qout", [128, 1024], BF16, st=stA1)
        qp2 = sb("qp2", [128, 1024], mybir.dt.int8, st=stA1)


        def ld(dst, src, tag):
            return sch.dma("sync", lambda e, d=dst, s=src: e.dma_start(out=d, in_=s),
                           sem="LS", tag=tag)

        ld(t_muk[:, :], d_muk, "muk")
        ld(t_amr[:, :], d_mur, "mur")
        ld(t_amv[:, :], d_muv, "muv")
        ld(t_xprev[:, :], d_xprev, "xprev")
        ld(t_u[:, :], d_uv, "u")
        ld(t_w[:, :], d_wv, "w")
        ld(t_b1[:, :], d_b1, "b1")
        ld(t_b2[:, :], d_b2, "b2")
        lx_setup = ld(t_maskt[:, :], d_mask, "mask")

        def gset(e):
            e.memset(t_cons[:, CONS_ZERO:CONS_ZERO + 1], 0.0)
            e.memset(t_cons[:, CONS_ONE:CONS_ONE + 1], 1.0)
            e.memset(t_cons[:, CONS_NEG1:CONS_NEG1 + 1], -1.0)
            e.memset(t_cons[:, CONS_HALF:CONS_HALF + 1], 0.5)
            e.memset(t_cons[:, CONS_NHALF:CONS_NHALF + 1], -0.5)
            e.memset(t_cons[:, CONS_INV2048:CONS_INV2048 + 1], 1.0 / 2048.0)
            e.memset(t_cons[:, CONS_EPS:CONS_EPS + 1], LN_EPS)
            e.memset(t_cons[:, CONS_NQF:CONS_NQF + 1], -0.7 / (8192.0 * 2048.0))
            e.memset(t_cons[:, CONS_NC0:CONS_NC0 + 1], -0.796875)
            e.memset(t_cons[:, CONS_C0:CONS_C0 + 1], 0.796875)
            e.memset(t_cons[:, CONS_INV224:CONS_INV224 + 1], 1.0 / 16777216.0)
            e.memset(t_cons[:, CONS_P07:CONS_P07 + 1], 0.7)
            e.memset(t_cons[:, CONS_BIAS:CONS_BIAS + 1], 1.0 + 2.0 ** -23)
            e.memset(t_onebf[:, :], 1.0)
            e.memset(t_onef[:, :], 1.0)
            e.memset(t_onerow[:, :], 1.0)
            e.memset(t_acol[:, :], 0.0)
            e.memset(t_bcol[:, :], 0.0)
            return e.iota(t_iota[:, :], pattern=[[1, TC]], base=0,
                          channel_multiplier=0, allow_small_or_imprecise_dtypes=True)
        ms_gset = sch.op("gpsimd", gset, inc=("GS", 1), tag="setup")

        ms_ew = sch.op("scalar",
                       lambda e: e.activation(t_weff[:, :], t_w[:, :], ACT.Exp),
                       wait=[("LS", lx_setup), ("GS", ms_gset)],
                       inc=("SS", 1), tag="exp_w")
        ms_weff = sch.op("vector",
                         lambda e: e.tensor_scalar_mul(t_weff[:, :], t_weff[:, :],
                                                       cons(CONS_NEG1)),
                         wait=[("SS", ms_ew)], inc=("VS", 1), tag="weff")
        ms_dcs = sch.op("scalar",
                        lambda e: e.activation(t_dcol[:, :], t_weff[:, :], ACT.Exp),
                        wait=[("VS", ms_weff)], inc=("SS", 1), tag="dcol")
        ms_sc1024 = sch.op("vector",
                           lambda e: e.tensor_scalar_mul(t_e1k[:, :], t_weff[:, :],
                                                         1024.0),
                           wait=[("VS", ms_weff)], inc=("VS", 1), tag="w1024")
        ms_e1k = sch.op("scalar",
                        lambda e: e.activation(t_e1k[:, :], t_e1k[:, :], ACT.Exp),
                        wait=[("VS", ms_sc1024)], inc=("SS", 1), tag="e1k")
        ms_e1km1 = sch.op("vector",
                          lambda e: e.tensor_scalar(t_e1km1[:, :], t_e1k[:, :],
                                                    cons(CONS_ONE), None, ALU.subtract),
                          wait=[("SS", ms_e1k)], inc=("VS", 1), tag="e1km1")
        ms_amk = sch.op("vector",
                        lambda e: e.tensor_scalar(t_amk[:, :], t_muk[:, :],
                                                  cons(CONS_ONE), None, ALU.subtract),
                        inc=("VS", 1), tag="amk")
        ms_amr = sch.op("vector",
                        lambda e: e.tensor_scalar(t_amr[:, :], t_amr[:, :],
                                                  cons(CONS_ONE), None, ALU.subtract),
                        inc=("VS", 1), tag="amr")
        ms_amv = sch.op("vector",
                        lambda e: e.tensor_scalar(t_amv[:, :], t_amv[:, :],
                                                  cons(CONS_ONE), None, ALU.subtract),
                        inc=("VS", 1), tag="amv")
        ms_setupv = max(ms_amk, ms_amr, ms_amv, ms_e1km1)

        def stream_blk(dsrc, blkidx, slot, waits, tag):
            sem = f"LW{slot}"
            val = sch.dma(
                "gpsimd",
                lambda e, d=dsrc, b=blkidx, s=slot: e.dma_start(
                    out=wpool[:, s].rearrange("p a b -> p (a b)"),
                    in_=d[b]),
                wait=waits, sem=sem, tag=tag)
            return (sem, val)

        qsrc = [( (d_w1s if i < 16 else d_w2s)[(i % 16) // 2]
                  [:, 1024 * ((i % 16) % 2):1024 * ((i % 16) % 2 + 1)] )
                for i in range(2 * NQB)]
        lw_idx = {}
        for cb in range(2):
            lw_idx[(cb, "k")] = stream_blk(d_wkb, cb, 2 * cb, [], f"wk{cb}")
            lw_idx[(cb, "v")] = stream_blk(d_wvb, cb, 2 * cb + 1, [], f"wv{cb}")

        gp_red = []
        ms_qst = {}
        for i in range(2 * NQB):
            qldw = [("SS", gp_red[-1])] if i > 0 else [("GS", ms_gset)]
            qld = sch.dma(
                "gpsimd",
                lambda e, s=qsrc[i]: e.dma_start(out=qring[:, :], in_=s),
                wait=qldw, sem="LQ", tag=f"qld{i}")
            qa32 = T3[:, 0, :]
            qs32 = T6[:, 4, :]
            absw = [("LQ", qld)]
            if i > 0:
                absw.append(("TS", ms_qst[i - 1]))

            def qabs(e, qa32=qa32, qs32=qs32):
                e.activation(qa32, qring[:, :], ACT.Abs)
                e.drain()
                return e.activation(qs32, qa32, ACT.Identity,
                                    bias=cons(CONS_NC0))
            gp_red.append(sch.op("scalar", qabs, wait=absw, inc=("SS", 1),
                                 tag=f"qabs{i}"))
            pst = big[2] if i < 16 else big[3]
            for q in range(2):
                ms_qst[i] = sch.mm(
                    "tensor",
                    lambda e, q=q, pst=pst, i=i, qs32=qs32: e.matmul(
                        pst[0:1, 0:TH], t_onef[:, 0:1],
                        qs32[:, TH * q:TH * (q + 1)],
                        start=(i % 16 == 0 and q == 0),
                        stop=(i % 16 == 15 and q == 1), skip_group_check=True),
                    wait=[("SS", gp_red[i])] if q == 0 else (),
                    inc=("TS", 1) if q == 1 else None, tag=f"qst{i}")
        ms_qstats = ms_qst[2 * NQB - 1]

        x_ld = []
        ms_mix = []
        for j in range(NCB):
            w = [("VS", ms_mix[j - 2])] if j >= 2 else []
            x_ld.append(sch.dma(
                "sync", lambda e, j=j: e.dma_start(out=xin[:, j % 2, :],
                                                   in_=d_x[128 * j:128 * (j + 1), :]),
                wait=w, sem=f"LX{j % 2}", tag=f"xld{j}"))
            xi = xin[:, j % 2, :]
            dd = T6[:, 0, :]

            def mix(e, j=j, xi=xi, dd=dd):
                e.tensor_sub(dd[:, 0:1], xi[:, 0:1], t_xprev[:, j:j + 1])
                e.drain()
                e.tensor_sub(dd[:, 1:TC], xi[:, 1:TC], xi[:, 0:TC - 1])
                e.drain()
                e.scalar_tensor_tensor(bfA[:, j, :], dd, t_amk[:, j:j + 1], xi,
                                       ALU.mult, ALU.add)
                e.drain()
                e.scalar_tensor_tensor(xrring[:, 0, :], dd, t_amr[:, j:j + 1], xi,
                                       ALU.mult, ALU.add)
                e.drain()
                return e.scalar_tensor_tensor(bfB[:, j, :], dd, t_amv[:, j:j + 1], xi,
                                              ALU.mult, ALU.add)
            mixw = [(f"LX{j % 2}", x_ld[j]), ("VS", ms_setupv)]
            if j >= 1:
                mixw.append(("XR", 16 * j))
            ms_mix.append(sch.op("vector", mix, wait=mixw, inc=("VS", 1),
                                 tag=f"mix{j}"))
            sch.dma("sync", lambda e, j=j: e.dma_start(
                out=d_xr[128 * j:128 * (j + 1), :], in_=xrring[:, 0, :]),
                wait=[("VS", ms_mix[j])], sem="XR", tag=f"xrst{j}")
        ms_mix_done = ms_mix[-1]

        def qrows(e):
            for mi, pst in ((0, big[2]), (1, big[3])):
                e.tensor_reduce(t_qsc[0:1, 16:48],
                                pst[0:1, 0:TH].rearrange("p (a b) -> p a b", a=32),
                                mybir.AxisListType.X, ALU.add)
                e.drain()
                e.tensor_reduce(t_qsc[0:1, mi:mi + 1], t_qsc[0:1, 16:48],
                                mybir.AxisListType.X, ALU.add)
                e.drain()
            return e.tensor_copy(t_qsc[0:1, 2:4], t_qsc[0:1, 0:2])
        ms_qrow = sch.op("vector", qrows, wait=[("TS", ms_qstats)],
                         inc=("VS", 1), tag="qrow")
        ms_sardma = sch.dma("sync",
                            lambda e: e.dma_start(out=d_sar_in, in_=t_qsc[0:1, 0:2]),
                            wait=[("VS", ms_qrow)], sem="SB", tag="sar_out")
        ms_ar = sch.op("gpsimd",
                       lambda e: e.collective_compute(
                           "AllReduce", ALU.add, replica_groups=[list(range(NCORES))],
                           ins=[d_sar_in], outs=[d_sar_out]),
                       wait=[("SB", ms_sardma)], inc=("CC", 1), drain=False,
                       tag="sAR")
        ms_sarback = sch.dma("sync",
                             lambda e: e.dma_start(out=t_qsc[0:1, 2:4], in_=d_sar_out),
                             wait=[("CC", ms_ar)], sem="SB", tag="sar_in")
        def taumath(e):
            e.tensor_scalar_mul(t_qsc[0:1, 4:6], t_qsc[0:1, 2:4],
                                cons(CONS_INV224)[0:1, :])
            e.drain()
            e.tensor_scalar(t_qsc[0:1, 12:14], t_qsc[0:1, 4:6],
                            cons(CONS_C0)[0:1, :], cons(CONS_BIAS)[0:1, :],
                            ALU.add, ALU.mult)
            e.drain()
            e.tensor_scalar_mul(t_qsc[0:1, 8:10], t_qsc[0:1, 12:14],
                                cons(CONS_P07)[0:1, :])
            e.drain()
            return e.tensor_scalar_mul(t_qsc[0:1, 10:12], t_qsc[0:1, 8:10],
                                       cons(CONS_NEG1)[0:1, :])
        ms_taumath = sch.op("vector", taumath, wait=[("SB", ms_sarback)],
                            inc=("VS", 1), tag="taumath")
        ms_taumm = sch.mm("tensor",
                          lambda e: e.matmul(big[2][:, 0:4], t_onerow[0:1, :],
                                             t_qsc[0:1, 8:12], start=True, stop=True),
                          wait=[("VS", ms_taumath)],
                          inc=("TS", 1), tag="taumm")
        ms_tau = sch.op("vector",
                        lambda e: e.tensor_copy(t_tau[:, 0:4], big[2][:, 0:4]),
                        wait=[("TS", ms_taumm)], inc=("VS", 1), tag="taup")

        t_mm_done = {}
        s_ekd = {}
        v_nd = {}
        v_done = {}
        wbf_ms = {}
        b_d2 = {}
        s_q2 = {}
        q2ld = {}
        qw = {}
        for cb in range(NCB):
            pk = big[cb % 2]
            pv = big[2 + cb % 2]
            kslot = (2 * cb) % 4
            vslot = (2 * cb + 1) % 4
            kw = [lw_idx[(cb, "k")]]
            if cb == 0:
                kw.append(("VS", ms_mix_done))
            if cb >= 2:
                kw.append(("SS", s_ekd[cb - 2]))
            for th in range(2):
                for j in range(NCB):
                    last = (th == 1 and j == NCB - 1)
                    r = sch.mm(
                        "tensor",
                        lambda e, th=th, j=j, pk=pk, kslot=kslot: e.matmul(
                            pk[:, TH * th:TH * (th + 1)],
                            wpool[:, kslot, j, :], bfA[:, j, TH * th:TH * (th + 1)],
                            start=(j == 0), stop=(j == NCB - 1),
                            skip_group_check=True),
                        wait=kw if (th == 0 and j == 0) else (),
                        inc=("TS", 1) if last else None, tag=f"kp{cb}")
            t_mm_done[(cb, "k")] = r
            vw = [lw_idx[(cb, "v")]]
            if cb in (0, 1):
                vw.append(("VS", ms_tau))
            if cb >= 2:
                vw.append(("VS", v_nd[cb - 2]))
            for th in range(2):
                for j in range(NCB):
                    last = (th == 1 and j == NCB - 1)
                    r = sch.mm(
                        "tensor",
                        lambda e, th=th, j=j, pv=pv, vslot=vslot: e.matmul(
                            pv[:, TH * th:TH * (th + 1)],
                            wpool[:, vslot, j, :], bfB[:, j, TH * th:TH * (th + 1)],
                            start=(j == 0), stop=(j == NCB - 1),
                            skip_group_check=True),
                        wait=vw if (th == 0 and j == 0) else (),
                        inc=("TS", 1) if last else None, tag=f"vp{cb}")
            t_mm_done[(cb, "v")] = r

            for wi in (2 * cb, 2 * cb + 1):
                srcb = d_wrb[wi] if wi < NCB else d_wob[wi - NCB]
                wbf_ms[wi] = sch.dma(
                    "gpsimd", lambda e, wi=wi, srcb=srcb: e.dma_start(
                        out=d_wbf[wi], in_=srcb),
                    sem="WBF", tag=f"wbf{wi}")

            if cb + 2 < NCB:
                nb = cb + 2
                for which, dsrc in (("k", d_wkb), ("v", d_wvb)):
                    slot = (2 * nb + (0 if which == "k" else 1)) % 4
                    waits = [("TS", t_mm_done[(cb, which)])]
                    lw_idx[(nb, which)] = stream_blk(dsrc, nb, slot, waits,
                                                     f"w{which}{nb}")

            p = cb % 2
            ek = T6[:, p, :]
            ed = T6[:, 2 + p, :]
            sw = [("TS", t_mm_done[(cb, "k")])]
            if cb >= 2:
                sw.append(("VS", v_done[cb - 2]))
            sch.op("scalar",
                   lambda e, pk=pk, ek=ek: e.activation(ek, pk[:, :], ACT.Exp),
                   wait=sw, tag=f"ek{cb}")
            s_ekd[cb] = sch.op("scalar",
                               lambda e, pk=pk, ed=ed, cb=cb: e.activation(
                                   ed, pk[:, :], ACT.Exp, bias=t_u[:, cb:cb + 1]),
                               inc=("SS", 1), tag=f"ed{cb}")

            z = T6[:, 4, :]
            t2 = T6[:, 5, :]
            AA = T3[:, 0, :]
            BB = T3[:, 1, :]
            dbc = t_dcol[:, cb:cb + 1].broadcast_to([128, TC])
            zc = cons(CONS_ZERO)

            zw = [("SS", s_ekd[cb]), ("TS", t_mm_done[(cb, "v")])]
            if cb == 0:
                zw.append(("TS", ms_qstats))
            sch.op("vector",
                   lambda e, ek=ek, pv=pv, z=z: e.tensor_mul(z, ek, pv[:, :]),
                   wait=zw, tag=f"z{cb}")
            v_nd[cb] = sch.op("vector",
                              lambda e, ed=ed, pv=pv, t2=t2: e.tensor_mul(t2, ed, pv[:, :]),
                              inc=("VS", 1), tag=f"t2{cb}")
            sch.op("vector",
                   lambda e, dbc=dbc, z=z, AA=AA, zc=zc: e.tensor_tensor_scan(
                       AA, dbc, z, zc, ALU.mult, ALU.add),
                   tag=f"aa{cb}")

            def vnum(e, AA=AA, t2=t2, cb=cb):
                e.tensor_copy(NUM[:, cb, 0:1], t2[:, 0:1])
                e.drain()
                e.tensor_add(NUM[:, cb, 1:TC], AA[:, 0:TC - 1], t2[:, 1:TC])
                e.drain()
                return e.tensor_copy(t_sums[:, cb:cb + 1], AA[:, TC - 1:TC])
            sch.op("vector", vnum, tag=f"num{cb}")
            sch.op("vector",
                   lambda e, dbc=dbc, ek=ek, BB=BB, zc=zc: e.tensor_tensor_scan(
                       BB, dbc, ek, zc, ALU.mult, ALU.add),
                   tag=f"bb{cb}")

            def vden(e, BB=BB, ed=ed, cb=cb):
                e.tensor_copy(DEN[:, cb, 0:1], ed[:, 0:1])
                e.drain()
                e.tensor_add(DEN[:, cb, 1:TC], BB[:, 0:TC - 1], ed[:, 1:TC])
                e.drain()
                return e.tensor_copy(t_sums[:, NCB + cb:NCB + cb + 1], BB[:, TC - 1:TC])
            v_aden = sch.op("vector", vden, inc=("VS", 1), tag=f"den{cb}")
            v_done[cb] = v_aden

        ms_A_done = v_aden

        ms_sumst = sch.dma("sync",
                           lambda e: e.dma_start(out=d_sums_in, in_=t_sums[:, :]),
                           wait=[("VS", ms_A_done)], sem="SB", tag="sums_out")
        ms_sag = sch.op("gpsimd",
                        lambda e: e.collective_compute(
                            "AllGather", ALU.bypass,
                            replica_groups=[list(range(NCORES))],
                            ins=[d_sums_in], outs=[d_sums_out]),
                        wait=[("SB", ms_sumst)], inc=("CC", 1), drain=False,
                        tag="sAG")
        ms_gathld = sch.dma("sync",
                            lambda e: e.dma_start(
                                out=t_gath[:, :, :],
                                in_=d_sums_out.rearrange("(r p) c -> p r c", p=128)),
                            wait=[("CC", ms_sag)], sem="SB", tag="gath")

        def horner(e):
            for r in range(NCORES):
                e.tensor_scalar(t_dm[:, :], t_e1km1[:, :], t_maskt[:, r:r + 1],
                                cons(CONS_ONE), ALU.mult, ALU.add)
                e.drain()
                for (acc, off) in ((t_acol, 0), (t_bcol, NCB)):
                    e.tensor_mul(acc[:, :], acc[:, :], t_dm[:, :])
                    e.drain()
                    e.tensor_scalar_mul(t_hs[:, :], t_gath[:, r, off:off + NCB],
                                        t_maskt[:, r:r + 1])
                    e.drain()
                    e.tensor_add(acc[:, :], acc[:, :], t_hs[:, :])
                    e.drain()
            return e.tensor_copy(t_hs[:, 0:1], t_acol[:, 0:1])
        ms_horner = sch.op("vector", horner,
                           wait=[("SB", ms_gathld)], inc=("VS", 1), tag="horner")

        if stage == "a":
            _debug_ab(sch, stage, dbg, NUM, DEN, t_sums, t_acol, t_bcol,
                      ms_A_done, ms_A_done, ms_horner, cnt)
            stA1.close()
            stA2.close()
            _finish(nc, sch, ctx)
            return nc

        def stream_blk_sync(srcidx, slot, waits, tag):
            sem = f"LB{slot}"
            val = sch.dma(
                "scalar",
                lambda e, i=srcidx, s=slot: e.dma_start(
                    out=wpool[:, s].rearrange("p a b -> p (a b)"),
                    in_=d_wbf[i]),
                wait=waits + [("WBF", wbf_ms[srcidx])], sem=sem, tag=tag)
            return (sem, val)

        xr16 = bfA
        xr_ld = []
        for j in range(NCB):
            w = []
            if j == 0:
                w = [("XR", 16 * NCB), ("TS", t_mm_done[(NCB - 1, "k")])]
            xr_ld.append(sch.dma(
                "sync", lambda e, j=j: e.dma_start(
                    out=xr16[:, j, :], in_=d_xr[128 * j:128 * (j + 1), :]),
                wait=w, sem="XB", tag=f"xrld{j}"))

        b_mm_done = {}
        b_s_done = {}
        b_v_done = {}
        for cb in range(NCB):
            slot = cb % 4
            if cb >= 4:
                ww = [("TS", b_mm_done[cb - 4])]
            else:
                lastu = {0: (14, "k"), 1: (14, "v"), 2: (15, "k"), 3: (15, "v")}[slot]
                ww = [("TS", t_mm_done[lastu])]
            lwr = stream_blk_sync(cb, slot, ww, f"wr{cb}")
            pr = big[cb % 4]
            rw = [lwr, ("XB", xr_ld[-1])]
            if cb in (0, 1):
                rw.append(("VS", ms_A_done))
            if cb >= 4:
                rw.append(("SS", b_s_done[cb - 4]))
            for th in range(2):
                for j in range(NCB):
                    last = (th == 1 and j == NCB - 1)
                    r = sch.mm(
                        "tensor",
                        lambda e, th=th, j=j, pr=pr, slot=slot: e.matmul(
                            pr[:, TH * th:TH * (th + 1)],
                            wpool[:, slot, j, :], xr16[:, j, TH * th:TH * (th + 1)],
                            start=(j == 0), stop=(j == NCB - 1),
                            skip_group_check=True),
                        wait=rw if (th == 0 and j == 0) else (),
                        inc=("TS", 1) if last else None, tag=f"rp{cb}")
            b_mm_done[cb] = r

            p = cb % 2
            PW = T3[:, p, :]
            NUMf = T6[:, 3 * p + 0, :]
            DV = T6[:, 3 * p + 1, :]
            ER = T6[:, 3 * p + 2, :]

            w = [("VS", ms_horner)]
            if cb >= 2:
                w.append(("SS", b_s_done[cb - 2]))
            ms_pwv = sch.op("vector",
                            lambda e, PW=PW, cb=cb: e.tensor_scalar_mul(
                                PW, t_iota[:, :], t_weff[:, cb:cb + 1]),
                            wait=w, inc=("VS", 1), tag=f"pwv{cb}")
            ms_pws = sch.op("scalar",
                            lambda e, PW=PW: e.activation(PW, PW, ACT.Exp),
                            wait=[("VS", ms_pwv)], inc=("SS", 1), tag=f"pws{cb}")
            erw = [("TS", b_mm_done[cb])]
            if cb >= 2:
                erw.append(("VS", b_d2[cb - 2]))
            ms_er = sch.op("scalar",
                           lambda e, pr=pr, ER=ER: e.activation(
                               ER, pr[:, :], ACT.Exp, scale=cons(CONS_NEG1)),
                           wait=erw, inc=("SS", 1), tag=f"er{cb}")

            sch.op("vector",
                   lambda e, PW=PW, NUMf=NUMf, cb=cb: e.scalar_tensor_tensor(
                       NUMf, PW, t_acol[:, cb:cb + 1], NUM[:, cb, :],
                       ALU.mult, ALU.add),
                   wait=[("SS", ms_pws)], tag=f"numf{cb}")
            sch.op("vector",
                   lambda e, PW=PW, DV=DV, cb=cb: e.scalar_tensor_tensor(
                       DV, PW, t_bcol[:, cb:cb + 1], DEN[:, cb, :],
                       ALU.mult, ALU.add),
                   tag=f"denf{cb}")
            sch.op("vector",
                   lambda e, ER=ER: e.tensor_scalar(ER, ER, cons(CONS_ONE), None,
                                                    ALU.add),
                   wait=[("SS", ms_er)], tag=f"den2{cb}")
            b_d2[cb] = sch.op("vector",
                              lambda e, DV=DV, ER=ER: e.tensor_mul(DV, DV, ER),
                              inc=("VS", 1), tag=f"d2{cb}")
            sch.op("scalar",
                   lambda e, DV=DV, ER=ER: e.activation(ER, DV, ACT.Ln),
                   wait=[("VS", b_d2[cb])], tag=f"ln{cb}")
            b_s_done[cb] = sch.op("scalar",
                                  lambda e, ER=ER, DV=DV: e.activation(
                                      DV, ER, ACT.Exp, scale=cons(CONS_NEG1)),
                                  inc=("SS", 1), tag=f"rinv{cb}")
            b_v_done[cb] = sch.op("vector",
                                  lambda e, NUMf=NUMf, DV=DV, cb=cb: e.tensor_mul(
                                      NUM[:, cb, :], NUMf, DV),
                                  wait=[("SS", b_s_done[cb])], inc=("VS", 1),
                                  tag=f"s{cb}")

            for i in (2 * cb, 2 * cb + 1):
                dqin = d_q1in if i < 16 else d_q2in
                bi = (i % 16) // 2
                hf = (i % 16) % 2
                tcol = 0 if i < 16 else 1
                qw2w = [("VS", ms_tau)]
                if i == 0:
                    qw2w.append(("SS", gp_red[-1]))
                else:
                    qw2w.append(("VS", s_q2[i - 1]))
                q2ld[i] = sch.dma(
                    "sync",
                    lambda e, s=qsrc[i]: e.dma_start(out=qring[:, :], in_=s),
                    wait=qw2w, sem="LQ2", tag=f"q2ld{i}")

                def qtern(e, i=i, tcol=tcol):
                    e.tensor_single_scalar(qout[:, :], qring[:, :],
                                           t_tau[:, 2 + tcol:3 + tcol], ALU.is_le)
                    e.drain()
                    return e.scalar_tensor_tensor(qp2[:, :], qring[:, :],
                                                  t_tau[:, tcol:tcol + 1],
                                                  qout[:, :], ALU.is_ge,
                                                  ALU.subtract)
                qtw = [("LQ2", q2ld[i])]
                if i > 0:
                    qtw.append(("QW", qw[i - 1]))
                s_q2[i] = sch.op("vector", qtern, wait=qtw, inc=("VS", 1),
                                 tag=f"qtern{i}")
                qw[i] = sch.dma(
                    "sync",
                    lambda e, d=dqin, b=bi, hf=hf: e.dma_start(
                        out=d[b][:, 1024 * hf:1024 * (hf + 1)],
                        in_=qp2[:, :]),
                    wait=[("VS", s_q2[i])], sem="QW", tag=f"qw_{i}")




        ms_B_done = b_v_done[NCB - 1]

        ms_ag1 = sch.op("gpsimd",
                        lambda e: e.collective_compute(
                            "AllGather", ALU.bypass,
                            replica_groups=[list(range(NCORES))],
                            ins=[d_q1in], outs=[d_q1all]),
                        wait=[("QW", cnt["QW"])], inc=("CC", 1), drain=False,
                        tag="qAG1")
        ms_ag2 = sch.op("gpsimd",
                        lambda e: e.collective_compute(
                            "AllGather", ALU.bypass,
                            replica_groups=[list(range(NCORES))],
                            ins=[d_q2in], outs=[d_q2all]),
                        inc=("CC", 1), drain=False, tag="qAG2")


        if stage == "b":
            _debug_ab(sch, stage, dbg, NUM, DEN, t_sums, t_acol, t_bcol,
                      ms_A_done, ms_B_done, ms_horner, cnt)
            stA1.close()
            stA2.close()
            _finish(nc, sch, ctx)
            return nc

        stA1.close()
        stC = ExitStack()
        sqb = sb("sqb", [128, TC], BF16, st=stC)
        row_rstd = xin[0:1, 0, :]
        row_mr = xin[0:1, 1, :]

        S = NUM
        c_mm_done = {}
        c_v_done = {}
        c_st_done = {}
        x2_ld = []
        for cb in range(NCB):
            slot = cb % 4
            ww = [("TS", c_mm_done[cb - 4] if cb >= 4 else b_mm_done[NCB - 1])]
            lwo = stream_blk_sync(NCB + cb, slot, ww, f"wo{cb}")
            w = [("VS", c_v_done[cb - 2])] if cb >= 2 else []
            x2_ld.append(sch.dma(
                "sync", lambda e, cb=cb: e.dma_start(
                    out=xin[:, cb % 2, :], in_=d_x[128 * cb:128 * (cb + 1), :]),
                wait=w, sem=f"XC{cb % 2}", tag=f"x2ld{cb}"))
            po = big[cb % 2]
            mw = [lwo, ("VS", ms_B_done)]
            if cb >= 2:
                mw.append(("VS", c_v_done[cb - 2]))
            for th in range(2):
                for j in range(NCB):
                    last = (th == 1 and j == NCB - 1)
                    r = sch.mm(
                        "tensor",
                        lambda e, th=th, j=j, po=po, slot=slot: e.matmul(
                            po[:, TH * th:TH * (th + 1)],
                            wpool[:, slot, j, :], S[:, j, TH * th:TH * (th + 1)],
                            start=(j == 0), stop=(j == NCB - 1),
                            skip_group_check=True),
                        wait=mw if (th == 0 and j == 0) else (),
                        inc=("TS", 1) if last else None, tag=f"wo{cb}")
            c_mm_done[cb] = r

            def vres(e, po=po, cb=cb):
                e.tensor_add(bfA[:, cb, :], po[:, :], xin[:, cb % 2, :])
                e.drain()
                return e.tensor_mul(sqb[:, :], bfA[:, cb, :], bfA[:, cb, :])
            vw = [("TS", c_mm_done[cb]), (f"XC{cb % 2}", x2_ld[cb])]
            if cb >= 1:
                vw.append(("TS", c_st_done[cb - 1]))
            c_v_done[cb] = sch.op("vector", vres, wait=vw, inc=("VS", 1),
                                  tag=f"res{cb}")
            for th in range(2):
                sch.mm("tensor",
                       lambda e, th=th, cb=cb: e.matmul(
                           big[2][0:1, TH * th:TH * (th + 1)], t_onebf[:, 0:1],
                           bfA[:, cb, TH * th:TH * (th + 1)],
                           start=(cb == 0), stop=(cb == NCB - 1),
                           skip_group_check=True),
                       wait=[("VS", c_v_done[cb])] if th == 0 else (),
                       tag=f"st1_{cb}")
                c_st_done[cb] = sch.mm(
                    "tensor",
                    lambda e, th=th, cb=cb: e.matmul(
                        big[3][0:1, TH * th:TH * (th + 1)], t_onebf[:, 0:1],
                        sqb[:, TH * th:TH * (th + 1)],
                        start=(cb == 0), stop=(cb == NCB - 1),
                        skip_group_check=True),
                    inc=("TS", 1) if th == 1 else None, tag=f"st2_{cb}")
        ms_stats1 = c_st_done[NCB - 1]

        def ln_rows(e):
            e.tensor_scalar_mul(row_rstd, big[2][0:1, :],
                                cons(CONS_INV2048)[0:1, :])
            e.drain()
            e.tensor_scalar_mul(row_mr, big[3][0:1, :],
                                cons(CONS_INV2048)[0:1, :])
            e.drain()
            e.tensor_mul(big[2][32:33, :], row_rstd, row_rstd)
            e.drain()
            e.tensor_sub(big[3][32:33, :], row_mr, big[2][32:33, :])
            e.drain()
            return e.tensor_scalar(big[3][64:65, :], big[3][32:33, :],
                                   cons(CONS_EPS)[0:1, :], None, ALU.add)
        ms_lnr = sch.op("vector", ln_rows, wait=[("TS", ms_stats1)],
                        inc=("VS", 1), tag="ln1rows")

        def ln_rstd(e):
            e.activation(big[2][64:65, :], big[3][64:65, :], ACT.Ln)
            e.drain()
            return e.activation(big[2][32:33, :], big[2][64:65, :], ACT.Exp,
                                scale=cons(CONS_NHALF)[0:1, :])
        ms_rstd = sch.op("scalar", ln_rstd, wait=[("VS", ms_lnr)],
                         inc=("SS", 1), tag="ln1rstd")

        def ln_mr(e):
            e.tensor_mul(row_mr, row_rstd, big[2][32:33, :])
            e.drain()
            return e.tensor_copy(row_rstd, big[2][32:33, :])
        ms_mr = sch.op("vector", ln_mr,
                       wait=[("SS", ms_rstd)], inc=("VS", 1), tag="ln1mr")
        bc_ms = None
        for th in range(2):
            sch.mm("tensor",
                   lambda e, th=th: e.matmul(
                       big[2][:, TH * th:TH * (th + 1)], t_onerow[0:1, :],
                       row_rstd[:, TH * th:TH * (th + 1)], start=True, stop=True),
                   wait=[("VS", ms_mr)] if th == 0 else (), tag="bc1")
            bc_ms = sch.mm("tensor",
                           lambda e, th=th: e.matmul(
                               big[3][:, TH * th:TH * (th + 1)], t_onerow[0:1, :],
                               row_mr[:, TH * th:TH * (th + 1)],
                               start=True, stop=True),
                           inc=("TS", 1) if th == 1 else None, tag="bc2")

        ap_ms = {}
        for cb in range(NCB):
            def vap(e, cb=cb):
                e.tensor_mul(sqb[:, :], bfA[:, cb, :], big[2][:, :])
                e.drain()
                return e.tensor_sub(bfB[:, cb, :], sqb[:, :], big[3][:, :])
            ap_ms[cb] = sch.op("vector", vap,
                               wait=[("TS", bc_ms)] if cb == 0 else (),
                               inc=("VS", 1), tag=f"ap1_{cb}")
        ms_C_done = ap_ms[NCB - 1]

        if stage == "c":
            _debug_c(sch, dbg, bfA, bfB, ms_C_done, cnt)
            stC.close()
            stA2.close()
            _finish(nc, sch, ctx)
            return nc

        stC.close()
        stA2.close()
        stF = ctx.enter_context(ExitStack())
        hbuf = sb("hbuf", [128, NFB, TH], BF16, st=stF)
        out2 = sb("out2", [128, NCB, TH], F32, st=stF)
        ln2s = sb("ln2s", [128, TH], F32, st=stF)
        sq2 = sb("sq2", [128, TH], BF16, st=stF)
        wpx = sb("wpx", [128, 4, NCB, 128], BF16, st=stF)

        def fslot(s):
            return wpool[:, s, :, :] if s < 4 else wpx[:, s - 4, :, :]

        prev_f2_v = None
        prev_sh_last = None
        prev_f2_ts = None
        out_st = []
        fslot_user = {0: c_mm_done[NCB - 4], 1: c_mm_done[NCB - 3],
                      2: c_mm_done[NCB - 2], 3: c_mm_done[NCB - 1]}
        for th in range(2):
            f1_done = {}
            s_h = {}
            for fc in range(NFB):
                slot = fc % 8
                ww = []
                if slot in fslot_user:
                    ww.append(("TS", fslot_user[slot]))
                if th == 0 and fc == 0:
                    ww.append(("CC", ms_ag1))
                lf = (f"LF{slot}", sch.dma("gpsimd",
                             lambda e, fc=fc, slot=slot: e.dma_start(
                                 out=fslot(slot).rearrange("p a b -> p (a b)"),
                                 in_=d_q1all[fc]),
                             wait=ww, sem=f"LF{slot}", tag=f"f1w{th}_{fc}"))
                ph = big[0][:, TH * (fc % 2):TH * (fc % 2 + 1)]
                mw = [lf]
                if th == 0 and fc == 0:
                    mw.append(("VS", ms_C_done))
                if prev_sh_last is not None and fc == 0:
                    mw.append(("SS", prev_sh_last))
                if fc >= 2:
                    mw.append(("SS", s_h[fc - 2]))
                for j in range(NCB):
                    r = sch.mm(
                        "tensor",
                        lambda e, j=j, ph=ph, slot=slot, th=th: e.matmul(
                            ph, fslot(slot)[:, j, :],
                            bfB[:, j, TH * th:TH * (th + 1)],
                            start=(j == 0), stop=(j == NCB - 1),
                            skip_group_check=True),
                        wait=mw if j == 0 else (),
                        inc=("TS", 1) if j == NCB - 1 else None,
                        tag=f"f1_{th}_{fc}")
                f1_done[fc] = r
                fslot_user[slot] = r
                s_h[fc] = sch.op(
                    "scalar",
                    lambda e, ph=ph, fc=fc: e.activation(
                        hbuf[:, fc, :], ph, ACT.Gelu, bias=t_b1[:, fc:fc + 1]),
                    wait=[("TS", f1_done[fc])], inc=("SS", 1), tag=f"gelu{th}_{fc}")

            f2_done = {}
            f2v = {}
            for dc in range(NCB):
                qslots = [4 * (dc % 2) + q for q in range(4)]
                lf2s = []
                for q in range(4):
                    slot = qslots[q]
                    ww = [("TS", fslot_user[slot])] if slot in fslot_user else []
                    if th == 0 and dc == 0 and q == 0:
                        ww.append(("CC", ms_ag2))
                    gi = 8 * (dc // 2) + 4 * (dc % 2) + q
                    lf2s.append((f"LF{slot}", sch.dma("gpsimd",
                                  lambda e, gi=gi, slot=slot: e.dma_start(
                                      out=fslot(slot).rearrange("p a b -> p (a b)"),
                                      in_=d_q2all[gi]),
                                  wait=ww, sem=f"LF{slot}", tag=f"f2w{th}_{dc}_{q}")))
                po2 = big[1][:, TH * (dc % 2):TH * (dc % 2 + 1)]
                mw = lf2s + [("SS", s_h[NFB - 1])]
                if dc >= 2:
                    mw.append(("VS", f2v[dc - 2]))
                for q in range(4):
                    slot = qslots[q]
                    for j in range(NCB):
                        fj = 16 * q + j
                        r = sch.mm(
                            "tensor",
                            lambda e, q=q, j=j, po2=po2, slot=slot, fj=fj: e.matmul(
                                po2, fslot(slot)[:, j, :], hbuf[:, fj, :],
                                start=(fj == 0), stop=(fj == NFB - 1),
                                skip_group_check=True),
                            wait=mw if (q == 0 and j == 0) else (),
                            inc=("TS", 1) if j == NCB - 1 else None,
                            tag=f"f2_{th}_{dc}_{q}")
                    fslot_user[slot] = r
                f2_done[dc] = r

                def vevac(e, po2=po2, dc=dc, th=th):
                    e.tensor_scalar(out2[:, dc, :], po2, t_b2[:, dc:dc + 1],
                                    None, ALU.add)
                    e.drain()
                    e.tensor_add(out2[:, dc, :], out2[:, dc, :],
                                 bfB[:, dc, TH * th:TH * (th + 1)])
                    e.drain()
                    return e.tensor_mul(sq2[:, :], out2[:, dc, :], out2[:, dc, :])
                vw = [("TS", f2_done[dc])]
                if dc >= 1:
                    vw.append(("TS", c_st_done[("f2", th, dc - 1)]))
                if th == 1 and dc == 0:
                    vw.append(("SO", out_st[-1]))
                f2v[dc] = sch.op("vector", vevac, wait=vw, inc=("VS", 1),
                                 tag=f"f2v{th}_{dc}")
                sch.mm("tensor",
                       lambda e, dc=dc: e.matmul(
                           big[2][0:1, 0:TH], t_onef[:, 0:1], out2[:, dc, :],
                           start=(dc == 0), stop=(dc == NCB - 1),
                           skip_group_check=True),
                       wait=[("VS", f2v[dc])], tag=f"st3_{th}_{dc}")
                c_st_done[("f2", th, dc)] = sch.mm(
                    "tensor",
                    lambda e, dc=dc: e.matmul(
                        big[3][0:1, 0:TH], t_onebf[:, 0:1], sq2[:, :],
                        start=(dc == 0), stop=(dc == NCB - 1),
                        skip_group_check=True),
                    inc=("TS", 1), tag=f"st4_{th}_{dc}")
            ms_stats2 = c_st_done[("f2", th, NCB - 1)]

            def ln2_rows(e):
                e.tensor_scalar_mul(row_rstd[:, 0:TH], big[2][0:1, 0:TH],
                                    cons(CONS_INV2048)[0:1, :])
                e.drain()
                e.tensor_scalar_mul(row_mr[:, 0:TH], big[3][0:1, 0:TH],
                                    cons(CONS_INV2048)[0:1, :])
                e.drain()
                e.tensor_mul(big[2][32:33, 0:TH], row_rstd[:, 0:TH],
                             row_rstd[:, 0:TH])
                e.drain()
                e.tensor_sub(big[3][32:33, 0:TH], row_mr[:, 0:TH],
                             big[2][32:33, 0:TH])
                e.drain()
                return e.tensor_scalar(big[3][64:65, 0:TH], big[3][32:33, 0:TH],
                                       cons(CONS_EPS)[0:1, :], None, ALU.add)
            ms_lnr2 = sch.op("vector", ln2_rows, wait=[("TS", ms_stats2)],
                             inc=("VS", 1), tag=f"ln2rows{th}")

            def ln2_rstd(e):
                e.activation(big[2][64:65, 0:TH], big[3][64:65, 0:TH], ACT.Ln)
                e.drain()
                return e.activation(big[2][32:33, 0:TH], big[2][64:65, 0:TH],
                                    ACT.Exp, scale=cons(CONS_NHALF)[0:1, :])
            ms_rstd2 = sch.op("scalar", ln2_rstd, wait=[("VS", ms_lnr2)],
                              inc=("SS", 1), tag=f"ln2rstd{th}")

            def ln2_mr(e):
                e.tensor_mul(row_mr[:, 0:TH], row_rstd[:, 0:TH],
                             big[2][32:33, 0:TH])
                e.drain()
                return e.tensor_copy(row_rstd[:, 0:TH], big[2][32:33, 0:TH])
            ms_mr2 = sch.op("vector", ln2_mr,
                            wait=[("SS", ms_rstd2)], inc=("VS", 1), tag=f"ln2mr{th}")
            sch.mm("tensor",
                   lambda e: e.matmul(big[2][:, 0:TH], t_onerow[0:1, :],
                                      row_rstd[:, 0:TH], start=True, stop=True),
                   wait=[("VS", ms_mr2)], tag=f"bc3_{th}")
            bc2_ms = sch.mm("tensor",
                            lambda e: e.matmul(big[3][:, 0:TH], t_onerow[0:1, :],
                                               row_mr[:, 0:TH],
                                               start=True, stop=True),
                            inc=("TS", 1), tag=f"bc4_{th}")
            last_ap = None
            for dc in range(NCB):
                def vap2(e, dc=dc):
                    e.tensor_mul(ln2s[:, :], out2[:, dc, :], big[2][:, 0:TH])
                    e.drain()
                    return e.tensor_sub(out2[:, dc, :], ln2s[:, :], big[3][:, 0:TH])
                last_ap = sch.op("vector", vap2,
                                 wait=[("TS", bc2_ms)] if dc == 0 else (),
                                 inc=("VS", 1), tag=f"ap2_{th}_{dc}")
                out_st.append(sch.dma(
                    "sync",
                    lambda e, dc=dc, th=th: e.dma_start(
                        out=d_out[128 * dc:128 * (dc + 1), TH * th:TH * (th + 1)],
                        in_=out2[:, dc, :]),
                    wait=[("VS", last_ap)], sem="SO", tag=f"ost{th}_{dc}"))
            prev_f2_v = last_ap
            prev_sh_last = s_h[NFB - 1]
            prev_f2_ts = f2_done[NCB - 1]

        if stage == "cf":
            dbgq1 = nc.dram_tensor("dbgq1", [NFB, C, 128], BF16, kind="ExternalOutput").ap()
            dbgq2 = nc.dram_tensor("dbgq2", [NFB, C, 128], BF16, kind="ExternalOutput").ap()
            sch.dma("gpsimd", lambda e: e.dma_start(out=dbgq1, in_=d_q1all),
                    wait=[("SO", out_st[-1])], sem="DB", tag="dbgq1")
            sch.dma("gpsimd", lambda e: e.dma_start(out=dbgq2, in_=d_q2all),
                    sem="DB", tag="dbgq2")
            dbgh = nc.dram_tensor("dbgh", [FF, TH], F32, kind="ExternalOutput").ap()
            for fc in range(NFB):
                sch.dma("gpsimd", lambda e, fc=fc: e.dma_start(
                    out=dbgh[128 * fc:128 * (fc + 1), :], in_=hbuf[:, fc, :]),
                    wait=[("SO", out_st[-1])], sem="DB", tag=f"dbgh{fc}")
            for j in range(NCB):
                sch.dma("gpsimd", lambda e, j=j: e.dma_start(
                    out=dbg[0][128 * j:128 * (j + 1), :], in_=bfA[:, j, :]),
                    wait=[("SO", out_st[-1])], sem="DB", tag=f"dbg0_{j}")
                sch.dma("gpsimd", lambda e, j=j: e.dma_start(
                    out=dbg[1][128 * j:128 * (j + 1), :], in_=bfB[:, j, :]),
                    sem="DB", tag=f"dbg1_{j}")
            sch.op("gpsimd", lambda e: e.nop(), wait=[("DB", cnt["DB"])],
                   drain=False, tag="dbg_wait")

        sch.op("sync", lambda e: e.nop(), wait=[("SO", out_st[-1])],
               drain=False, tag="final")

        _finish(nc, sch, ctx)
    return nc


def _finish(nc, sch, ctx):
    sems = {}
    names = sorted({s for ops in sch.ops.values() for (w, _, i, _, _) in ops
                    for s in ([x[0] for x in w] + ([i[0]] if i else []))})
    for n in names:
        sems[n] = ctx.enter_context(nc.semaphore(n))
    block = ctx.enter_context(nc.Block())

    def emit(engname):
        def run(eng):
            for wait, fn, inc, drain, tag in sch.ops[engname]:
                for s, v in wait:
                    eng.wait_ge(sems[s], v)
                inst = fn(eng)
                if drain:
                    d = eng.drain()
                    if inc is not None:
                        d.then_inc(sems[inc[0]], inc[1])
                elif inc is not None:
                    inst.then_inc(sems[inc[0]], inc[1])
        return run

    block.sync(emit("sync"))
    block.gpsimd(emit("gpsimd"))
    block.tensor(emit("tensor"))
    block.vector(emit("vector"))
    block.scalar(emit("scalar"))


def _debug_ab(sch, stage, dbg, NUM, DEN, t_sums, t_acol, t_bcol,
              ms_A_done, ms_B_done, ms_horner, cnt):
    ms = ms_A_done if stage == "a" else ms_B_done
    for j in range(NCB):
        sch.dma("gpsimd", lambda e, j=j: e.dma_start(
            out=dbg[0][128 * j:128 * (j + 1), :], in_=NUM[:, j, :]),
            wait=[("VS", ms)], sem="DB", tag=f"dbg0_{j}")
        sch.dma("gpsimd", lambda e, j=j: e.dma_start(
            out=dbg[1][128 * j:128 * (j + 1), :], in_=DEN[:, j, :]),
            sem="DB", tag=f"dbg1_{j}")
    sch.dma("gpsimd", lambda e: e.dma_start(out=dbg["s"][:, 0:32], in_=t_sums[:, :]),
            sem="DB", tag="dbgs")
    if stage == "b":
        sch.dma("gpsimd", lambda e: e.dma_start(out=dbg["s"][:, 32:48],
                                                in_=t_acol[:, :]),
                wait=[("VS", ms_horner)], sem="DB", tag="dbga")
        sch.dma("gpsimd", lambda e: e.dma_start(out=dbg["s"][:, 48:64],
                                                in_=t_bcol[:, :]),
                sem="DB", tag="dbgb")
    sch.op("gpsimd", lambda e: e.nop(), wait=[("DB", cnt["DB"])],
           drain=False, tag="dbg_wait")


def _debug_c(sch, dbg, bfA, bfB, ms_C_done, cnt):
    for j in range(NCB):
        sch.dma("gpsimd", lambda e, j=j: e.dma_start(
            out=dbg[0][128 * j:128 * (j + 1), :], in_=bfA[:, j, :]),
            wait=[("VS", ms_C_done)], sem="DB", tag=f"dbg0_{j}")
        sch.dma("gpsimd", lambda e, j=j: e.dma_start(
            out=dbg[1][128 * j:128 * (j + 1), :], in_=bfB[:, j, :]),
            sem="DB", tag=f"dbg1_{j}")
    sch.op("gpsimd", lambda e: e.nop(), wait=[("DB", cnt["DB"])],
           drain=False, tag="dbg_wait")



def _blocks(wT, nb):
    K, M = wT.shape
    assert M == nb * 128 and K % 128 == 0
    nj = K // 128
    b = wT.reshape(nj, 128, nb, 128)
    b = b.transpose(2, 1, 0, 3)
    return np.ascontiguousarray(b.reshape(nb, 128, K))


def _cols(v):
    return np.ascontiguousarray(np.asarray(v, dtype=np.float32).reshape(-1, 128).T)


def make_in_maps(inputs):
    x = np.ascontiguousarray(np.asarray(inputs["L0_attn_x"], dtype=np.float32)[0])
    Wr = np.asarray(inputs["L0_attn_Wr"], dtype=np.float32)
    Wk = np.asarray(inputs["L0_attn_Wk"], dtype=np.float32)
    Wv = np.asarray(inputs["L0_attn_Wv"], dtype=np.float32)
    Wo = np.asarray(inputs["L0_attn_Wo"], dtype=np.float32)
    w1 = np.asarray(inputs["L0_ffn_w1"], dtype=np.float32)
    w2 = np.asarray(inputs["L0_ffn_w2"], dtype=np.float32)

    wkb = _blocks(np.ascontiguousarray(Wk.T), NCB)
    wvb = _blocks(np.ascontiguousarray(Wv.T), NCB)
    wrb = _blocks(np.ascontiguousarray(Wr.T), NCB)
    wob = _blocks(np.ascontiguousarray(Wo.T), NCB)
    w1b = _blocks(np.ascontiguousarray(w1.T), NFB)
    w2tb = np.ascontiguousarray(w2.T).reshape(FF, NCB, 128)
    w2q = np.empty((NCB, 4, 128, 2048), dtype=np.float32)
    for dc in range(NCB):
        blk = w2tb[:, dc, :]
        q = blk.reshape(4, 16, 128, 128)
        w2q[dc] = q.transpose(0, 2, 1, 3).reshape(4, 128, 2048)

    muk = _cols(inputs["L0_attn_mu_k"])
    mur = _cols(inputs["L0_attn_mu_r"])
    muv = _cols(inputs["L0_attn_mu_v"])
    uv = _cols(inputs["L0_attn_wkv_u"])
    wv_ = _cols(inputs["L0_attn_wkv_w"])
    b1 = _cols(inputs["L0_ffn_b1"])
    b2 = _cols(inputs["L0_ffn_b2"])

    in_maps = []
    for c in range(NCORES):
        sl = slice(TC * c, TC * (c + 1))
        xc = np.ascontiguousarray(x[sl].T)
        if c == 0:
            xp = np.zeros((128, NCB), dtype=np.float32)
        else:
            xp = _cols(x[TC * c - 1])
        mask = np.zeros(NCORES, dtype=np.float32)
        mask[:c] = 1.0
        maskb = np.ascontiguousarray(np.broadcast_to(mask, (128, NCORES)))
        w1s = np.ascontiguousarray(w1b[8 * c:8 * (c + 1)])
        w2s = np.ascontiguousarray(w2q[2 * c:2 * (c + 1)].reshape(8, 128, 2048))
        in_maps.append({
            "x": xc, "xprev": xp, "wkb": wkb, "wvb": wvb, "wrb": wrb,
            "wob": wob, "w1s": w1s, "w2s": w2s, "muk": muk, "mur": mur,
            "muv": muv, "uv": uv, "wv": wv_, "b1": b1, "b2": b2,
            "mask": maskb,
        })
    return in_maps


def kernel(**inputs):
    in_maps = make_in_maps(inputs)
    nc = build_program("full")
    res = run_bass_kernel_spmd(nc, in_maps, core_ids=list(range(NCORES)))
    out = np.empty((NCORES * TC, C), dtype=np.float32)
    for c in range(NCORES):
        out[TC * c:TC * (c + 1), :] = res.results[c]["out"].T
    return out.reshape(1, NCORES * TC, C)


# version 19
# speedup vs baseline: 1.1302x; 1.0172x over previous
import numpy as np
import ml_dtypes
from collections import defaultdict
from contextlib import ExitStack

import concourse.bass as bass
import concourse.mybir as mybir
from concourse.bass_utils import run_bass_kernel_spmd

F32 = mybir.dt.float32
BF16 = mybir.dt.bfloat16
ALU = mybir.AluOpType
ACT = mybir.ActivationFunctionType

NCORES = 8
TC = 1024
C = 2048
NCB = 16
TH = 512
FF = 8192
NFB = 64
NQB = 16
LN_EPS = 1e-5

ENGINES = ("sync", "gpsimd", "tensor", "vector", "scalar")


class Sched:

    def __init__(self):
        self.ops = {e: [] for e in ENGINES}
        self.cnt = defaultdict(int)

    def op(self, eng, fn, wait=(), inc=None, drain=True, tag=""):
        for s, v in wait:
            assert isinstance(v, int), f"wait {s} is {v!r} at {tag}"
        after = None
        if inc is not None:
            s, a = inc
            self.cnt[s] += a
            after = self.cnt[s]
        self.ops[eng].append((tuple(wait), fn, inc, drain, tag))
        return after

    def dma(self, eng, fn, wait=(), sem="LD", tag=""):
        return self.op(eng, fn, wait=wait, inc=(sem, 16), drain=False, tag=tag)

    def mm(self, eng, fn, wait=(), inc=None, tag=""):
        return self.op(eng, fn, wait=wait, inc=inc, drain=False, tag=tag)


def build_program(stage="full"):
    nc = bass.Bass("TRN2", target_bir_lowering=False)

    d_x = nc.dram_tensor("x", [C, TC], F32, kind="ExternalInput").ap()
    d_xprev = nc.dram_tensor("xprev", [128, NCB], F32, kind="ExternalInput").ap()
    d_wkb = nc.dram_tensor("wkb", [NCB, 128, C], F32, kind="ExternalInput").ap()
    d_wvb = nc.dram_tensor("wvb", [NCB, 128, C], F32, kind="ExternalInput").ap()
    d_wrb = nc.dram_tensor("wrb", [NCB, 128, C], F32, kind="ExternalInput").ap()
    d_wob = nc.dram_tensor("wob", [NCB, 128, C], F32, kind="ExternalInput").ap()
    d_w1s = nc.dram_tensor("w1s", [8, 128, C], F32, kind="ExternalInput").ap()
    d_w2s = nc.dram_tensor("w2s", [8, 128, C], F32, kind="ExternalInput").ap()
    d_muk = nc.dram_tensor("muk", [128, NCB], F32, kind="ExternalInput").ap()
    d_mur = nc.dram_tensor("mur", [128, NCB], F32, kind="ExternalInput").ap()
    d_muv = nc.dram_tensor("muv", [128, NCB], F32, kind="ExternalInput").ap()
    d_uv = nc.dram_tensor("uv", [128, NCB], F32, kind="ExternalInput").ap()
    d_wv = nc.dram_tensor("wv", [128, NCB], F32, kind="ExternalInput").ap()
    d_b1 = nc.dram_tensor("b1", [128, NFB], F32, kind="ExternalInput").ap()
    d_b2 = nc.dram_tensor("b2", [128, NCB], F32, kind="ExternalInput").ap()
    d_mask = nc.dram_tensor("mask", [128, NCORES], F32, kind="ExternalInput").ap()
    d_out = nc.dram_tensor("out", [C, TC], F32, kind="ExternalOutput").ap()

    d_xr = nc.dram_tensor("xr_scr", [C, TC], BF16).ap()
    d_q1in = nc.dram_tensor("q1in", [8, 128, C], mybir.dt.int8).ap()
    d_q2in = nc.dram_tensor("q2in", [8, 128, C], mybir.dt.int8).ap()
    d_q1all = nc.dram_tensor("q1all", [NFB, 128, C], mybir.dt.int8).ap()
    d_q2all = nc.dram_tensor("q2all", [NFB, 128, C], mybir.dt.int8).ap()
    d_sar_in = nc.dram_tensor("sar_in", [1, 2], F32).ap()
    d_sar_out = nc.dram_tensor("sar_out", [1, 2], F32).ap()
    d_wbf = nc.dram_tensor("wbf", [2 * NCB, 128, C], BF16).ap()
    d_sums_in = nc.dram_tensor("sums_in", [128, 32], F32).ap()
    d_sums_out = nc.dram_tensor("sums_out", [128 * NCORES, 32], F32).ap()

    dbg = {}
    if stage not in ("full",):
        for i in range(2):
            dbg[i] = nc.dram_tensor(f"dbg{i}", [C, TC], F32, kind="ExternalOutput").ap()
        dbg["s"] = nc.dram_tensor("dbgs", [128, 64], F32, kind="ExternalOutput").ap()

    sch = Sched()
    cnt = sch.cnt

    with ExitStack() as ctx:
        def sb(name, shape, dt=F32, st=None):
            return (st or ctx).enter_context(nc.sbuf_tensor(name, shape, dt))

        def psum(name, shape):
            return ctx.enter_context(nc.psum_tensor(name, shape, F32))

        t_muk = sb("t_muk", [128, NCB])
        t_amk = sb("t_amk", [128, NCB])
        t_amr = sb("t_amr", [128, NCB])
        t_amv = sb("t_amv", [128, NCB])
        t_xprev = sb("t_xprev", [128, NCB])
        t_u = sb("t_u", [128, NCB])
        t_w = sb("t_w", [128, NCB])
        t_weff = sb("t_weff", [128, NCB])
        t_dcol = sb("t_dcol", [128, NCB])
        t_e1k = sb("t_e1k", [128, NCB])
        t_e1km1 = sb("t_e1km1", [128, NCB])
        t_acol = sb("t_acol", [128, NCB])
        t_bcol = sb("t_bcol", [128, NCB])
        t_dm = sb("t_dm", [128, NCB])
        t_hs = sb("t_hs", [128, NCB])
        t_sums = sb("t_sums", [128, 32])
        t_gath = sb("t_gath", [128, NCORES, 32])
        t_maskt = sb("t_maskt", [128, NCORES])
        t_b1 = sb("t_b1", [128, NFB])
        t_b2 = sb("t_b2", [128, NCB])
        t_cons = sb("t_cons", [128, 16])
        t_onebf = sb("t_onebf", [128, 1], BF16)
        t_onef = sb("t_onef", [128, 1], F32)
        t_onerow = sb("t_onerow", [1, 128], F32)
        t_qsc = sb("t_qsc", [1, 48])
        t_tau = sb("t_tau", [128, 4])
        bfA = sb("bfA", [128, NCB, TC], BF16)
        bfB = sb("bfB", [128, NCB, TC], BF16)
        wpool = sb("wpool", [128, 4, NCB, 128], BF16)
        xin = sb("xin", [128, 2, TC], F32)

        (CONS_ZERO, CONS_ONE, CONS_NEG1, CONS_HALF, CONS_NHALF, CONS_INV2048,
         CONS_EPS, CONS_NQF, CONS_NC0, CONS_C0, CONS_INV224, CONS_P07,
         CONS_BIAS) = range(13)

        def cons(i):
            return t_cons[:, i:i + 1]

        big = [psum(f"big{i}", [128, TC]) for i in range(4)]

        stA2 = ExitStack()
        NUM = sb("NUM", [128, NCB, TC], BF16, st=stA2)
        DEN = sb("DEN", [128, NCB, TC], BF16, st=stA2)
        stA1 = ExitStack()
        T6 = sb("T6", [128, 6, TC], F32, st=stA1)
        T3 = sb("T3", [128, 2, TC], F32, st=stA1)
        t_iota = sb("t_iota", [128, TC], F32, st=stA1)
        xrring = sb("xrring", [128, 1, TC], BF16, st=stA1)
        qring = sb("qring", [128, 1024], F32, st=stA1)
        qout = sb("qout", [128, 1024], BF16, st=stA1)
        qp2 = sb("qp2", [128, 1024], mybir.dt.int8, st=stA1)


        def ld(dst, src, tag):
            return sch.dma("sync", lambda e, d=dst, s=src: e.dma_start(out=d, in_=s),
                           sem="LS", tag=tag)

        ld(t_muk[:, :], d_muk, "muk")
        ld(t_amr[:, :], d_mur, "mur")
        ld(t_amv[:, :], d_muv, "muv")
        ld(t_xprev[:, :], d_xprev, "xprev")
        ld(t_u[:, :], d_uv, "u")
        ld(t_w[:, :], d_wv, "w")
        ld(t_b1[:, :], d_b1, "b1")
        ld(t_b2[:, :], d_b2, "b2")
        lx_setup = ld(t_maskt[:, :], d_mask, "mask")

        def gset(e):
            e.memset(t_cons[:, CONS_ZERO:CONS_ZERO + 1], 0.0)
            e.memset(t_cons[:, CONS_ONE:CONS_ONE + 1], 1.0)
            e.memset(t_cons[:, CONS_NEG1:CONS_NEG1 + 1], -1.0)
            e.memset(t_cons[:, CONS_HALF:CONS_HALF + 1], 0.5)
            e.memset(t_cons[:, CONS_NHALF:CONS_NHALF + 1], -0.5)
            e.memset(t_cons[:, CONS_INV2048:CONS_INV2048 + 1], 1.0 / 2048.0)
            e.memset(t_cons[:, CONS_EPS:CONS_EPS + 1], LN_EPS)
            e.memset(t_cons[:, CONS_NQF:CONS_NQF + 1], -0.7 / (8192.0 * 2048.0))
            e.memset(t_cons[:, CONS_NC0:CONS_NC0 + 1], -0.796875)
            e.memset(t_cons[:, CONS_C0:CONS_C0 + 1], 0.796875)
            e.memset(t_cons[:, CONS_INV224:CONS_INV224 + 1], 1.0 / 16777216.0)
            e.memset(t_cons[:, CONS_P07:CONS_P07 + 1], 0.7)
            e.memset(t_cons[:, CONS_BIAS:CONS_BIAS + 1], 1.0 + 2.0 ** -23)
            e.memset(t_onebf[:, :], 1.0)
            e.memset(t_onef[:, :], 1.0)
            e.memset(t_onerow[:, :], 1.0)
            e.memset(t_acol[:, :], 0.0)
            e.memset(t_bcol[:, :], 0.0)
            return e.iota(t_iota[:, :], pattern=[[1, TC]], base=0,
                          channel_multiplier=0, allow_small_or_imprecise_dtypes=True)
        ms_gset = sch.op("gpsimd", gset, inc=("GS", 1), tag="setup")

        ms_ew = sch.op("scalar",
                       lambda e: e.activation(t_weff[:, :], t_w[:, :], ACT.Exp),
                       wait=[("LS", lx_setup), ("GS", ms_gset)],
                       inc=("SS", 1), tag="exp_w")
        ms_weff = sch.op("vector",
                         lambda e: e.tensor_scalar_mul(t_weff[:, :], t_weff[:, :],
                                                       cons(CONS_NEG1)),
                         wait=[("SS", ms_ew)], inc=("VS", 1), tag="weff")
        ms_dcs = sch.op("scalar",
                        lambda e: e.activation(t_dcol[:, :], t_weff[:, :], ACT.Exp),
                        wait=[("VS", ms_weff)], inc=("SS", 1), tag="dcol")
        ms_sc1024 = sch.op("vector",
                           lambda e: e.tensor_scalar_mul(t_e1k[:, :], t_weff[:, :],
                                                         1024.0),
                           wait=[("VS", ms_weff)], inc=("VS", 1), tag="w1024")
        ms_e1k = sch.op("scalar",
                        lambda e: e.activation(t_e1k[:, :], t_e1k[:, :], ACT.Exp),
                        wait=[("VS", ms_sc1024)], inc=("SS", 1), tag="e1k")
        ms_e1km1 = sch.op("vector",
                          lambda e: e.tensor_scalar(t_e1km1[:, :], t_e1k[:, :],
                                                    cons(CONS_ONE), None, ALU.subtract),
                          wait=[("SS", ms_e1k)], inc=("VS", 1), tag="e1km1")
        ms_amk = sch.op("vector",
                        lambda e: e.tensor_scalar(t_amk[:, :], t_muk[:, :],
                                                  cons(CONS_ONE), None, ALU.subtract),
                        inc=("VS", 1), tag="amk")
        ms_amr = sch.op("vector",
                        lambda e: e.tensor_scalar(t_amr[:, :], t_amr[:, :],
                                                  cons(CONS_ONE), None, ALU.subtract),
                        inc=("VS", 1), tag="amr")
        ms_amv = sch.op("vector",
                        lambda e: e.tensor_scalar(t_amv[:, :], t_amv[:, :],
                                                  cons(CONS_ONE), None, ALU.subtract),
                        inc=("VS", 1), tag="amv")
        ms_setupv = max(ms_amk, ms_amr, ms_amv, ms_e1km1)

        def stream_blk(dsrc, blkidx, slot, waits, tag):
            sem = f"LW{slot}"
            val = sch.dma(
                "gpsimd",
                lambda e, d=dsrc, b=blkidx, s=slot: e.dma_start(
                    out=wpool[:, s].rearrange("p a b -> p (a b)"),
                    in_=d[b]),
                wait=waits, sem=sem, tag=tag)
            return (sem, val)

        qsrc = [( (d_w1s if i < 16 else d_w2s)[(i % 16) // 2]
                  [:, 1024 * ((i % 16) % 2):1024 * ((i % 16) % 2 + 1)] )
                for i in range(2 * NQB)]
        lw_idx = {}
        for cb in range(2):
            lw_idx[(cb, "k")] = stream_blk(d_wkb, cb, 2 * cb, [], f"wk{cb}")
            lw_idx[(cb, "v")] = stream_blk(d_wvb, cb, 2 * cb + 1, [], f"wv{cb}")

        gp_red = []
        ms_qst = {}
        for i in range(2 * NQB):
            qldw = [("SS", gp_red[-1])] if i > 0 else [("GS", ms_gset)]
            qld = sch.dma(
                "gpsimd",
                lambda e, s=qsrc[i]: e.dma_start(out=qring[:, :], in_=s),
                wait=qldw, sem="LQ", tag=f"qld{i}")
            qa32 = T3[:, 0, :]
            qs32 = T6[:, 4, :]
            absw = [("LQ", qld)]
            if i > 0:
                absw.append(("TS", ms_qst[i - 1]))

            def qabs(e, qa32=qa32, qs32=qs32):
                e.activation(qa32, qring[:, :], ACT.Abs)
                e.drain()
                return e.activation(qs32, qa32, ACT.Identity,
                                    bias=cons(CONS_NC0))
            gp_red.append(sch.op("scalar", qabs, wait=absw, inc=("SS", 1),
                                 tag=f"qabs{i}"))
            pst = big[2] if i < 16 else big[3]
            for q in range(2):
                ms_qst[i] = sch.mm(
                    "tensor",
                    lambda e, q=q, pst=pst, i=i, qs32=qs32: e.matmul(
                        pst[0:1, 0:TH], t_onef[:, 0:1],
                        qs32[:, TH * q:TH * (q + 1)],
                        start=(i % 16 == 0 and q == 0),
                        stop=(i % 16 == 15 and q == 1), skip_group_check=True),
                    wait=[("SS", gp_red[i])] if q == 0 else (),
                    inc=("TS", 1) if q == 1 else None, tag=f"qst{i}")
        ms_qstats = ms_qst[2 * NQB - 1]

        x_ld = []
        ms_mix = []
        for j in range(NCB):
            w = [("VS", ms_mix[j - 2])] if j >= 2 else []
            x_ld.append(sch.dma(
                "sync", lambda e, j=j: e.dma_start(out=xin[:, j % 2, :],
                                                   in_=d_x[128 * j:128 * (j + 1), :]),
                wait=w, sem=f"LX{j % 2}", tag=f"xld{j}"))
            xi = xin[:, j % 2, :]
            dd = T6[:, 0, :]

            def mix(e, j=j, xi=xi, dd=dd):
                e.tensor_sub(dd[:, 0:1], xi[:, 0:1], t_xprev[:, j:j + 1])
                e.drain()
                e.tensor_sub(dd[:, 1:TC], xi[:, 1:TC], xi[:, 0:TC - 1])
                e.drain()
                e.scalar_tensor_tensor(bfA[:, j, :], dd, t_amk[:, j:j + 1], xi,
                                       ALU.mult, ALU.add)
                e.drain()
                e.scalar_tensor_tensor(xrring[:, 0, :], dd, t_amr[:, j:j + 1], xi,
                                       ALU.mult, ALU.add)
                e.drain()
                return e.scalar_tensor_tensor(bfB[:, j, :], dd, t_amv[:, j:j + 1], xi,
                                              ALU.mult, ALU.add)
            mixw = [(f"LX{j % 2}", x_ld[j]), ("VS", ms_setupv)]
            if j >= 1:
                mixw.append(("XR", 16 * j))
            ms_mix.append(sch.op("vector", mix, wait=mixw, inc=("VS", 1),
                                 tag=f"mix{j}"))
            sch.dma("sync", lambda e, j=j: e.dma_start(
                out=d_xr[128 * j:128 * (j + 1), :], in_=xrring[:, 0, :]),
                wait=[("VS", ms_mix[j])], sem="XR", tag=f"xrst{j}")
        ms_mix_done = ms_mix[-1]

        def qrows(e):
            for mi, pst in ((0, big[2]), (1, big[3])):
                e.tensor_reduce(t_qsc[0:1, 16:48],
                                pst[0:1, 0:TH].rearrange("p (a b) -> p a b", a=32),
                                mybir.AxisListType.X, ALU.add)
                e.drain()
                e.tensor_reduce(t_qsc[0:1, mi:mi + 1], t_qsc[0:1, 16:48],
                                mybir.AxisListType.X, ALU.add)
                e.drain()
            return e.tensor_copy(t_qsc[0:1, 2:4], t_qsc[0:1, 0:2])
        ms_qrow = sch.op("vector", qrows, wait=[("TS", ms_qstats)],
                         inc=("VS", 1), tag="qrow")
        ms_sardma = sch.dma("sync",
                            lambda e: e.dma_start(out=d_sar_in, in_=t_qsc[0:1, 0:2]),
                            wait=[("VS", ms_qrow)], sem="SB", tag="sar_out")
        ms_ar = sch.op("gpsimd",
                       lambda e: e.collective_compute(
                           "AllReduce", ALU.add, replica_groups=[list(range(NCORES))],
                           ins=[d_sar_in], outs=[d_sar_out]),
                       wait=[("SB", ms_sardma)], inc=("CC", 1), drain=False,
                       tag="sAR")
        ms_sarback = sch.dma("sync",
                             lambda e: e.dma_start(out=t_qsc[0:1, 2:4], in_=d_sar_out),
                             wait=[("CC", ms_ar)], sem="SB", tag="sar_in")
        def taumath(e):
            e.tensor_scalar_mul(t_qsc[0:1, 4:6], t_qsc[0:1, 2:4],
                                cons(CONS_INV224)[0:1, :])
            e.drain()
            e.tensor_scalar(t_qsc[0:1, 12:14], t_qsc[0:1, 4:6],
                            cons(CONS_C0)[0:1, :], cons(CONS_BIAS)[0:1, :],
                            ALU.add, ALU.mult)
            e.drain()
            e.tensor_scalar_mul(t_qsc[0:1, 8:10], t_qsc[0:1, 12:14],
                                cons(CONS_P07)[0:1, :])
            e.drain()
            return e.tensor_scalar_mul(t_qsc[0:1, 10:12], t_qsc[0:1, 8:10],
                                       cons(CONS_NEG1)[0:1, :])
        ms_taumath = sch.op("vector", taumath, wait=[("SB", ms_sarback)],
                            inc=("VS", 1), tag="taumath")
        ms_taumm = sch.mm("tensor",
                          lambda e: e.matmul(big[2][:, 0:4], t_onerow[0:1, :],
                                             t_qsc[0:1, 8:12], start=True, stop=True),
                          wait=[("VS", ms_taumath)],
                          inc=("TS", 1), tag="taumm")
        ms_tau = sch.op("vector",
                        lambda e: e.tensor_copy(t_tau[:, 0:4], big[2][:, 0:4]),
                        wait=[("TS", ms_taumm)], inc=("VS", 1), tag="taup")

        t_mm_done = {}
        s_ekd = {}
        v_nd = {}
        v_done = {}
        wbf_ms = {}
        b_d2 = {}
        s_q2 = {}
        q2ld = {}
        qw = {}
        for cb in range(NCB):
            pk = big[cb % 2]
            pv = big[2 + cb % 2]
            kslot = (2 * cb) % 4
            vslot = (2 * cb + 1) % 4
            kw = [lw_idx[(cb, "k")]]
            if cb == 0:
                kw.append(("VS", ms_mix_done))
            if cb >= 2:
                kw.append(("SS", s_ekd[cb - 2]))
            for th in range(2):
                for j in range(NCB):
                    last = (th == 1 and j == NCB - 1)
                    r = sch.mm(
                        "tensor",
                        lambda e, th=th, j=j, pk=pk, kslot=kslot: e.matmul(
                            pk[:, TH * th:TH * (th + 1)],
                            wpool[:, kslot, j, :], bfA[:, j, TH * th:TH * (th + 1)],
                            start=(j == 0), stop=(j == NCB - 1),
                            skip_group_check=True),
                        wait=kw if (th == 0 and j == 0) else (),
                        inc=("TS", 1) if last else None, tag=f"kp{cb}")
            t_mm_done[(cb, "k")] = r
            vw = [lw_idx[(cb, "v")]]
            if cb in (0, 1):
                vw.append(("VS", ms_tau))
            if cb >= 2:
                vw.append(("VS", v_nd[cb - 2]))
            for th in range(2):
                for j in range(NCB):
                    last = (th == 1 and j == NCB - 1)
                    r = sch.mm(
                        "tensor",
                        lambda e, th=th, j=j, pv=pv, vslot=vslot: e.matmul(
                            pv[:, TH * th:TH * (th + 1)],
                            wpool[:, vslot, j, :], bfB[:, j, TH * th:TH * (th + 1)],
                            start=(j == 0), stop=(j == NCB - 1),
                            skip_group_check=True),
                        wait=vw if (th == 0 and j == 0) else (),
                        inc=("TS", 1) if last else None, tag=f"vp{cb}")
            t_mm_done[(cb, "v")] = r

            for wi in (2 * cb, 2 * cb + 1):
                srcb = d_wrb[wi] if wi < NCB else d_wob[wi - NCB]
                wbf_ms[wi] = sch.dma(
                    "gpsimd", lambda e, wi=wi, srcb=srcb: e.dma_start(
                        out=d_wbf[wi], in_=srcb),
                    sem="WBF", tag=f"wbf{wi}")

            if cb + 2 < NCB:
                nb = cb + 2
                for which, dsrc in (("k", d_wkb), ("v", d_wvb)):
                    slot = (2 * nb + (0 if which == "k" else 1)) % 4
                    waits = [("TS", t_mm_done[(cb, which)])]
                    lw_idx[(nb, which)] = stream_blk(dsrc, nb, slot, waits,
                                                     f"w{which}{nb}")

            p = cb % 2
            ek = T6[:, p, :]
            ed = T6[:, 2 + p, :]
            sw = [("TS", t_mm_done[(cb, "k")])]
            if cb >= 2:
                sw.append(("VS", v_done[cb - 2]))
            sch.op("scalar",
                   lambda e, pk=pk, ek=ek: e.activation(ek, pk[:, :], ACT.Exp),
                   wait=sw, tag=f"ek{cb}")
            s_ekd[cb] = sch.op("scalar",
                               lambda e, pk=pk, ed=ed, cb=cb: e.activation(
                                   ed, pk[:, :], ACT.Exp, bias=t_u[:, cb:cb + 1]),
                               inc=("SS", 1), tag=f"ed{cb}")

            z = T6[:, 4, :]
            t2 = T6[:, 5, :]
            AA = T3[:, 0, :]
            BB = T3[:, 1, :]
            dbc = t_dcol[:, cb:cb + 1].broadcast_to([128, TC])
            zc = cons(CONS_ZERO)

            zw = [("SS", s_ekd[cb]), ("TS", t_mm_done[(cb, "v")])]
            if cb == 0:
                zw.append(("TS", ms_qstats))
            sch.op("vector",
                   lambda e, ek=ek, pv=pv, z=z: e.tensor_mul(z, ek, pv[:, :]),
                   wait=zw, tag=f"z{cb}")
            v_nd[cb] = sch.op("vector",
                              lambda e, ed=ed, pv=pv, t2=t2: e.tensor_mul(t2, ed, pv[:, :]),
                              inc=("VS", 1), tag=f"t2{cb}")
            sch.op("vector",
                   lambda e, dbc=dbc, z=z, AA=AA, zc=zc: e.tensor_tensor_scan(
                       AA, dbc, z, zc, ALU.mult, ALU.add),
                   tag=f"aa{cb}")

            def vnum(e, AA=AA, t2=t2, cb=cb):
                e.tensor_copy(NUM[:, cb, 0:1], t2[:, 0:1])
                e.drain()
                e.tensor_add(NUM[:, cb, 1:TC], AA[:, 0:TC - 1], t2[:, 1:TC])
                e.drain()
                return e.tensor_copy(t_sums[:, cb:cb + 1], AA[:, TC - 1:TC])
            sch.op("vector", vnum, tag=f"num{cb}")
            sch.op("vector",
                   lambda e, dbc=dbc, ek=ek, BB=BB, zc=zc: e.tensor_tensor_scan(
                       BB, dbc, ek, zc, ALU.mult, ALU.add),
                   tag=f"bb{cb}")

            def vden(e, BB=BB, ed=ed, cb=cb):
                e.tensor_copy(DEN[:, cb, 0:1], ed[:, 0:1])
                e.drain()
                e.tensor_add(DEN[:, cb, 1:TC], BB[:, 0:TC - 1], ed[:, 1:TC])
                e.drain()
                return e.tensor_copy(t_sums[:, NCB + cb:NCB + cb + 1], BB[:, TC - 1:TC])
            v_aden = sch.op("vector", vden, inc=("VS", 1), tag=f"den{cb}")
            v_done[cb] = v_aden

        ms_A_done = v_aden

        ms_sumst = sch.dma("sync",
                           lambda e: e.dma_start(out=d_sums_in, in_=t_sums[:, :]),
                           wait=[("VS", ms_A_done)], sem="SB", tag="sums_out")
        ms_sag = sch.op("gpsimd",
                        lambda e: e.collective_compute(
                            "AllGather", ALU.bypass,
                            replica_groups=[list(range(NCORES))],
                            ins=[d_sums_in], outs=[d_sums_out]),
                        wait=[("SB", ms_sumst)], inc=("CC", 1), drain=False,
                        tag="sAG")
        ms_gathld = sch.dma("sync",
                            lambda e: e.dma_start(
                                out=t_gath[:, :, :],
                                in_=d_sums_out.rearrange("(r p) c -> p r c", p=128)),
                            wait=[("CC", ms_sag)], sem="SB", tag="gath")

        def horner(e):
            for r in range(NCORES):
                e.tensor_scalar(t_dm[:, :], t_e1km1[:, :], t_maskt[:, r:r + 1],
                                cons(CONS_ONE), ALU.mult, ALU.add)
                e.drain()
                for (acc, off) in ((t_acol, 0), (t_bcol, NCB)):
                    e.tensor_mul(acc[:, :], acc[:, :], t_dm[:, :])
                    e.drain()
                    e.tensor_scalar_mul(t_hs[:, :], t_gath[:, r, off:off + NCB],
                                        t_maskt[:, r:r + 1])
                    e.drain()
                    e.tensor_add(acc[:, :], acc[:, :], t_hs[:, :])
                    e.drain()
            return e.tensor_copy(t_hs[:, 0:1], t_acol[:, 0:1])
        ms_horner = sch.op("vector", horner,
                           wait=[("SB", ms_gathld)], inc=("VS", 1), tag="horner")

        if stage == "a":
            _debug_ab(sch, stage, dbg, NUM, DEN, t_sums, t_acol, t_bcol,
                      ms_A_done, ms_A_done, ms_horner, cnt)
            stA1.close()
            stA2.close()
            _finish(nc, sch, ctx)
            return nc

        def stream_blk_sync(srcidx, slot, waits, tag):
            sem = f"LB{slot}"
            val = sch.dma(
                "scalar",
                lambda e, i=srcidx, s=slot: e.dma_start(
                    out=wpool[:, s].rearrange("p a b -> p (a b)"),
                    in_=d_wbf[i]),
                wait=waits + [("WBF", wbf_ms[srcidx])], sem=sem, tag=tag)
            return (sem, val)

        xr16 = bfA
        xr_ld = []
        for j in range(NCB):
            w = []
            if j == 0:
                w = [("XR", 16 * NCB), ("TS", t_mm_done[(NCB - 1, "k")])]
            xr_ld.append(sch.dma(
                "sync", lambda e, j=j: e.dma_start(
                    out=xr16[:, j, :], in_=d_xr[128 * j:128 * (j + 1), :]),
                wait=w, sem="XB", tag=f"xrld{j}"))

        b_mm_done = {}
        b_s_done = {}
        b_v_done = {}
        for cb in range(NCB):
            slot = cb % 4
            if cb >= 4:
                ww = [("TS", b_mm_done[cb - 4])]
            else:
                lastu = {0: (14, "k"), 1: (14, "v"), 2: (15, "k"), 3: (15, "v")}[slot]
                ww = [("TS", t_mm_done[lastu])]
            lwr = stream_blk_sync(cb, slot, ww, f"wr{cb}")
            pr = big[cb % 4]
            rw = [lwr, ("XB", xr_ld[-1])]
            if cb in (0, 1):
                rw.append(("VS", ms_A_done))
            if cb >= 4:
                rw.append(("SS", b_s_done[cb - 4]))
            for th in range(2):
                for j in range(NCB):
                    last = (th == 1 and j == NCB - 1)
                    r = sch.mm(
                        "tensor",
                        lambda e, th=th, j=j, pr=pr, slot=slot: e.matmul(
                            pr[:, TH * th:TH * (th + 1)],
                            wpool[:, slot, j, :], xr16[:, j, TH * th:TH * (th + 1)],
                            start=(j == 0), stop=(j == NCB - 1),
                            skip_group_check=True),
                        wait=rw if (th == 0 and j == 0) else (),
                        inc=("TS", 1) if last else None, tag=f"rp{cb}")
            b_mm_done[cb] = r

            p = cb % 2
            PW = T3[:, p, :]
            NUMf = T6[:, 3 * p + 0, :]
            DV = T6[:, 3 * p + 1, :]
            ER = T6[:, 3 * p + 2, :]

            w = [("VS", ms_horner)]
            if cb >= 2:
                w.append(("SS", b_s_done[cb - 2]))
            ms_pwv = sch.op("vector",
                            lambda e, PW=PW, cb=cb: e.tensor_scalar_mul(
                                PW, t_iota[:, :], t_weff[:, cb:cb + 1]),
                            wait=w, inc=("VS", 1), tag=f"pwv{cb}")
            ms_pws = sch.op("scalar",
                            lambda e, PW=PW: e.activation(PW, PW, ACT.Exp),
                            wait=[("VS", ms_pwv)], inc=("SS", 1), tag=f"pws{cb}")
            erw = [("TS", b_mm_done[cb])]
            if cb >= 2:
                erw.append(("VS", b_d2[cb - 2]))
            ms_er = sch.op("scalar",
                           lambda e, pr=pr, ER=ER: e.activation(
                               ER, pr[:, :], ACT.Exp, scale=cons(CONS_NEG1)),
                           wait=erw, inc=("SS", 1), tag=f"er{cb}")

            sch.op("vector",
                   lambda e, PW=PW, NUMf=NUMf, cb=cb: e.scalar_tensor_tensor(
                       NUMf, PW, t_acol[:, cb:cb + 1], NUM[:, cb, :],
                       ALU.mult, ALU.add),
                   wait=[("SS", ms_pws)], tag=f"numf{cb}")
            sch.op("vector",
                   lambda e, PW=PW, DV=DV, cb=cb: e.scalar_tensor_tensor(
                       DV, PW, t_bcol[:, cb:cb + 1], DEN[:, cb, :],
                       ALU.mult, ALU.add),
                   tag=f"denf{cb}")
            sch.op("vector",
                   lambda e, ER=ER: e.tensor_scalar(ER, ER, cons(CONS_ONE), None,
                                                    ALU.add),
                   wait=[("SS", ms_er)], tag=f"den2{cb}")
            b_d2[cb] = sch.op("vector",
                              lambda e, DV=DV, ER=ER: e.tensor_mul(DV, DV, ER),
                              inc=("VS", 1), tag=f"d2{cb}")
            sch.op("scalar",
                   lambda e, DV=DV, ER=ER: e.activation(ER, DV, ACT.Ln),
                   wait=[("VS", b_d2[cb])], tag=f"ln{cb}")
            b_s_done[cb] = sch.op("scalar",
                                  lambda e, ER=ER, DV=DV: e.activation(
                                      DV, ER, ACT.Exp, scale=cons(CONS_NEG1)),
                                  inc=("SS", 1), tag=f"rinv{cb}")
            b_v_done[cb] = sch.op("vector",
                                  lambda e, NUMf=NUMf, DV=DV, cb=cb: e.tensor_mul(
                                      NUM[:, cb, :], NUMf, DV),
                                  wait=[("SS", b_s_done[cb])], inc=("VS", 1),
                                  tag=f"s{cb}")

            for i in (2 * cb, 2 * cb + 1):
                dqin = d_q1in if i < 16 else d_q2in
                bi = (i % 16) // 2
                hf = (i % 16) % 2
                tcol = 0 if i < 16 else 1
                qw2w = [("VS", ms_tau)]
                if i == 0:
                    qw2w.append(("SS", gp_red[-1]))
                else:
                    qw2w.append(("VS", s_q2[i - 1]))
                q2ld[i] = sch.dma(
                    "sync",
                    lambda e, s=qsrc[i]: e.dma_start(out=qring[:, :], in_=s),
                    wait=qw2w, sem="LQ2", tag=f"q2ld{i}")

                def qtern(e, i=i, tcol=tcol):
                    e.tensor_single_scalar(qout[:, :], qring[:, :],
                                           t_tau[:, 2 + tcol:3 + tcol], ALU.is_le)
                    e.drain()
                    return e.scalar_tensor_tensor(qp2[:, :], qring[:, :],
                                                  t_tau[:, tcol:tcol + 1],
                                                  qout[:, :], ALU.is_ge,
                                                  ALU.subtract)
                qtw = [("LQ2", q2ld[i])]
                if i > 0:
                    qtw.append(("QW", qw[i - 1]))
                s_q2[i] = sch.op("vector", qtern, wait=qtw, inc=("VS", 1),
                                 tag=f"qtern{i}")
                qw[i] = sch.dma(
                    "sync",
                    lambda e, d=dqin, b=bi, hf=hf: e.dma_start(
                        out=d[b][:, 1024 * hf:1024 * (hf + 1)],
                        in_=qp2[:, :]),
                    wait=[("VS", s_q2[i])], sem="QW", tag=f"qw_{i}")




        ms_B_done = b_v_done[NCB - 1]

        ms_ag1 = sch.op("gpsimd",
                        lambda e: e.collective_compute(
                            "AllGather", ALU.bypass,
                            replica_groups=[list(range(NCORES))],
                            ins=[d_q1in], outs=[d_q1all]),
                        wait=[("QW", cnt["QW"])], inc=("CC", 1), drain=False,
                        tag="qAG1")
        ms_ag2 = sch.op("gpsimd",
                        lambda e: e.collective_compute(
                            "AllGather", ALU.bypass,
                            replica_groups=[list(range(NCORES))],
                            ins=[d_q2in], outs=[d_q2all]),
                        inc=("CC", 1), drain=False, tag="qAG2")


        if stage == "b":
            _debug_ab(sch, stage, dbg, NUM, DEN, t_sums, t_acol, t_bcol,
                      ms_A_done, ms_B_done, ms_horner, cnt)
            stA1.close()
            stA2.close()
            _finish(nc, sch, ctx)
            return nc

        stA1.close()
        stC = ExitStack()
        sqb = sb("sqb", [128, TC], BF16, st=stC)
        sqg = sb("sqg", [128, TC], BF16, st=stC)
        rbt = sb("rbt", [128, TC], F32, st=stC)
        mbt = sb("mbt", [128, TC], F32, st=stC)
        row_rstd = xin[0:1, 0, :]
        row_mr = xin[0:1, 1, :]

        S = NUM
        c_mm_done = {}
        c_v_done = {}
        c_st_done = {}
        x2_ld = []
        for cb in range(NCB):
            slot = cb % 4
            ww = [("TS", c_mm_done[cb - 4] if cb >= 4 else b_mm_done[NCB - 1])]
            lwo = stream_blk_sync(NCB + cb, slot, ww, f"wo{cb}")
            w = [("VS", c_v_done[cb - 2])] if cb >= 2 else []
            x2_ld.append(sch.dma(
                "sync", lambda e, cb=cb: e.dma_start(
                    out=xin[:, cb % 2, :], in_=d_x[128 * cb:128 * (cb + 1), :]),
                wait=w, sem=f"XC{cb % 2}", tag=f"x2ld{cb}"))
            po = big[cb % 2]
            mw = [lwo, ("VS", ms_B_done)]
            if cb >= 2:
                mw.append(("VS", c_v_done[cb - 2]))
            for th in range(2):
                for j in range(NCB):
                    last = (th == 1 and j == NCB - 1)
                    r = sch.mm(
                        "tensor",
                        lambda e, th=th, j=j, po=po, slot=slot: e.matmul(
                            po[:, TH * th:TH * (th + 1)],
                            wpool[:, slot, j, :], S[:, j, TH * th:TH * (th + 1)],
                            start=(j == 0), stop=(j == NCB - 1),
                            skip_group_check=True),
                        wait=mw if (th == 0 and j == 0) else (),
                        inc=("TS", 1) if last else None, tag=f"wo{cb}")
            c_mm_done[cb] = r

            def vres(e, po=po, cb=cb):
                e.tensor_add(bfA[:, cb, :], po[:, :], xin[:, cb % 2, :])
                e.drain()
                return e.tensor_mul(sqb[:, :], bfA[:, cb, :], bfA[:, cb, :])
            vw = [("TS", c_mm_done[cb]), (f"XC{cb % 2}", x2_ld[cb])]
            if cb >= 1:
                vw.append(("TS", c_st_done[cb - 1]))
            c_v_done[cb] = sch.op("vector", vres, wait=vw, inc=("VS", 1),
                                  tag=f"res{cb}")
            for th in range(2):
                sch.mm("tensor",
                       lambda e, th=th, cb=cb: e.matmul(
                           big[2][0:1, TH * th:TH * (th + 1)], t_onebf[:, 0:1],
                           bfA[:, cb, TH * th:TH * (th + 1)],
                           start=(cb == 0), stop=(cb == NCB - 1),
                           skip_group_check=True),
                       wait=[("VS", c_v_done[cb])] if th == 0 else (),
                       tag=f"st1_{cb}")
                c_st_done[cb] = sch.mm(
                    "tensor",
                    lambda e, th=th, cb=cb: e.matmul(
                        big[3][0:1, TH * th:TH * (th + 1)], t_onebf[:, 0:1],
                        sqb[:, TH * th:TH * (th + 1)],
                        start=(cb == 0), stop=(cb == NCB - 1),
                        skip_group_check=True),
                    inc=("TS", 1) if th == 1 else None, tag=f"st2_{cb}")
        ms_stats1 = c_st_done[NCB - 1]

        def ln_rows(e):
            e.tensor_scalar_mul(row_rstd, big[2][0:1, :],
                                cons(CONS_INV2048)[0:1, :])
            e.drain()
            e.tensor_scalar_mul(row_mr, big[3][0:1, :],
                                cons(CONS_INV2048)[0:1, :])
            e.drain()
            e.tensor_mul(big[2][32:33, :], row_rstd, row_rstd)
            e.drain()
            e.tensor_sub(big[3][32:33, :], row_mr, big[2][32:33, :])
            e.drain()
            return e.tensor_scalar(big[3][64:65, :], big[3][32:33, :],
                                   cons(CONS_EPS)[0:1, :], None, ALU.add)
        ms_lnr = sch.op("vector", ln_rows, wait=[("TS", ms_stats1)],
                        inc=("VS", 1), tag="ln1rows")

        def ln_rstd(e):
            e.activation(big[2][64:65, :], big[3][64:65, :], ACT.Ln)
            e.drain()
            return e.activation(big[2][32:33, :], big[2][64:65, :], ACT.Exp,
                                scale=cons(CONS_NHALF)[0:1, :])
        ms_rstd = sch.op("scalar", ln_rstd, wait=[("VS", ms_lnr)],
                         inc=("SS", 1), tag="ln1rstd")

        def ln_mr(e):
            e.tensor_mul(row_mr, row_rstd, big[2][32:33, :])
            e.drain()
            return e.tensor_copy(row_rstd, big[2][32:33, :])
        ms_mr = sch.op("vector", ln_mr,
                       wait=[("SS", ms_rstd)], inc=("VS", 1), tag="ln1mr")
        bc_ms = None
        for th in range(2):
            sch.mm("tensor",
                   lambda e, th=th: e.matmul(
                       big[2][:, TH * th:TH * (th + 1)], t_onerow[0:1, :],
                       row_rstd[:, TH * th:TH * (th + 1)], start=True, stop=True),
                   wait=[("VS", ms_mr)] if th == 0 else (), tag="bc1")
            bc_ms = sch.mm("tensor",
                           lambda e, th=th: e.matmul(
                               big[3][:, TH * th:TH * (th + 1)], t_onerow[0:1, :],
                               row_mr[:, TH * th:TH * (th + 1)],
                               start=True, stop=True),
                           inc=("TS", 1) if th == 1 else None, tag="bc2")

        sch.op("vector", lambda e: e.tensor_copy(rbt[:, :], big[2][:, :]),
               wait=[("TS", bc_ms)], tag="rbc")
        ms_mbc = sch.op("vector", lambda e: e.tensor_copy(mbt[:, :], big[3][:, :]),
                        inc=("VS", 1), tag="mbc")
        ap_ms = {}
        gp_last = None
        for cb in range(NCB):
            eng = "gpsimd" if cb % 3 == 2 else "vector"
            scr = sqb if eng == "vector" else sqg

            def vap(e, cb=cb, scr=scr):
                e.tensor_mul(scr[:, :], bfA[:, cb, :], rbt[:, :])
                e.drain()
                return e.tensor_sub(bfB[:, cb, :], scr[:, :], mbt[:, :])
            w = [("VS", ms_mbc)]
            m = sch.op(eng, vap, wait=w,
                       inc=(("VS", 1) if eng == "vector" else ("GP", 1)),
                       tag=f"ap1_{cb}")
            ap_ms[cb] = m
            if eng == "gpsimd":
                gp_last = m
        ms_C_done = ap_ms[NCB - 1]
        ms_C_done_g = gp_last

        if stage == "c":
            _debug_c(sch, dbg, bfA, bfB, ms_C_done, cnt)
            stC.close()
            stA2.close()
            _finish(nc, sch, ctx)
            return nc

        stC.close()
        stA2.close()
        stF = ctx.enter_context(ExitStack())
        hbuf = sb("hbuf", [128, NFB, TH], BF16, st=stF)
        out2 = sb("out2", [128, NCB, TH], F32, st=stF)
        ln2s = sb("ln2s", [128, TH], F32, st=stF)
        sq2 = sb("sq2", [128, TH], BF16, st=stF)
        wpx = sb("wpx", [128, 4, NCB, 128], BF16, st=stF)

        def fslot(s):
            return wpool[:, s, :, :] if s < 4 else wpx[:, s - 4, :, :]

        prev_f2_v = None
        prev_sh_last = None
        prev_f2_ts = None
        out_st = []
        fslot_user = {0: c_mm_done[NCB - 4], 1: c_mm_done[NCB - 3],
                      2: c_mm_done[NCB - 2], 3: c_mm_done[NCB - 1]}
        for th in range(2):
            f1_done = {}
            s_h = {}
            for fc in range(NFB):
                slot = fc % 8
                ww = []
                if slot in fslot_user:
                    ww.append(("TS", fslot_user[slot]))
                if th == 0 and fc == 0:
                    ww.append(("CC", ms_ag1))
                lf = (f"LF{slot}", sch.dma("gpsimd",
                             lambda e, fc=fc, slot=slot: e.dma_start(
                                 out=fslot(slot).rearrange("p a b -> p (a b)"),
                                 in_=d_q1all[fc]),
                             wait=ww, sem=f"LF{slot}", tag=f"f1w{th}_{fc}"))
                ph = big[0][:, TH * (fc % 2):TH * (fc % 2 + 1)]
                mw = [lf]
                if th == 0 and fc == 0:
                    mw.append(("VS", ms_C_done))
                    mw.append(("GP", ms_C_done_g))
                if prev_sh_last is not None and fc == 0:
                    mw.append(("SS", prev_sh_last))
                if fc >= 2:
                    mw.append(("SS", s_h[fc - 2]))
                for j in range(NCB):
                    r = sch.mm(
                        "tensor",
                        lambda e, j=j, ph=ph, slot=slot, th=th: e.matmul(
                            ph, fslot(slot)[:, j, :],
                            bfB[:, j, TH * th:TH * (th + 1)],
                            start=(j == 0), stop=(j == NCB - 1),
                            skip_group_check=True),
                        wait=mw if j == 0 else (),
                        inc=("TS", 1) if j == NCB - 1 else None,
                        tag=f"f1_{th}_{fc}")
                f1_done[fc] = r
                fslot_user[slot] = r
                s_h[fc] = sch.op(
                    "scalar",
                    lambda e, ph=ph, fc=fc: e.activation(
                        hbuf[:, fc, :], ph, ACT.Gelu, bias=t_b1[:, fc:fc + 1]),
                    wait=[("TS", f1_done[fc])], inc=("SS", 1), tag=f"gelu{th}_{fc}")

            f2_done = {}
            f2v = {}
            for dc in range(NCB):
                qslots = [4 * (dc % 2) + q for q in range(4)]
                lf2s = []
                for q in range(4):
                    slot = qslots[q]
                    ww = [("TS", fslot_user[slot])] if slot in fslot_user else []
                    if th == 0 and dc == 0 and q == 0:
                        ww.append(("CC", ms_ag2))
                    gi = 8 * (dc // 2) + 4 * (dc % 2) + q
                    lf2s.append((f"LF{slot}", sch.dma("gpsimd",
                                  lambda e, gi=gi, slot=slot: e.dma_start(
                                      out=fslot(slot).rearrange("p a b -> p (a b)"),
                                      in_=d_q2all[gi]),
                                  wait=ww, sem=f"LF{slot}", tag=f"f2w{th}_{dc}_{q}")))
                po2 = big[1][:, TH * (dc % 2):TH * (dc % 2 + 1)]
                mw = lf2s + [("SS", s_h[NFB - 1])]
                if dc >= 2:
                    mw.append(("VS", f2v[dc - 2]))
                for q in range(4):
                    slot = qslots[q]
                    for j in range(NCB):
                        fj = 16 * q + j
                        r = sch.mm(
                            "tensor",
                            lambda e, q=q, j=j, po2=po2, slot=slot, fj=fj: e.matmul(
                                po2, fslot(slot)[:, j, :], hbuf[:, fj, :],
                                start=(fj == 0), stop=(fj == NFB - 1),
                                skip_group_check=True),
                            wait=mw if (q == 0 and j == 0) else (),
                            inc=("TS", 1) if j == NCB - 1 else None,
                            tag=f"f2_{th}_{dc}_{q}")
                    fslot_user[slot] = r
                f2_done[dc] = r

                def vevac(e, po2=po2, dc=dc, th=th):
                    e.tensor_scalar(out2[:, dc, :], po2, t_b2[:, dc:dc + 1],
                                    None, ALU.add)
                    e.drain()
                    e.tensor_add(out2[:, dc, :], out2[:, dc, :],
                                 bfB[:, dc, TH * th:TH * (th + 1)])
                    e.drain()
                    return e.tensor_mul(sq2[:, :], out2[:, dc, :], out2[:, dc, :])
                vw = [("TS", f2_done[dc])]
                if dc >= 1:
                    vw.append(("TS", c_st_done[("f2", th, dc - 1)]))
                if th == 1 and dc == 0:
                    vw.append(("SO", out_st[-1]))
                f2v[dc] = sch.op("vector", vevac, wait=vw, inc=("VS", 1),
                                 tag=f"f2v{th}_{dc}")
                sch.mm("tensor",
                       lambda e, dc=dc: e.matmul(
                           big[2][0:1, 0:TH], t_onef[:, 0:1], out2[:, dc, :],
                           start=(dc == 0), stop=(dc == NCB - 1),
                           skip_group_check=True),
                       wait=[("VS", f2v[dc])], tag=f"st3_{th}_{dc}")
                c_st_done[("f2", th, dc)] = sch.mm(
                    "tensor",
                    lambda e, dc=dc: e.matmul(
                        big[3][0:1, 0:TH], t_onebf[:, 0:1], sq2[:, :],
                        start=(dc == 0), stop=(dc == NCB - 1),
                        skip_group_check=True),
                    inc=("TS", 1), tag=f"st4_{th}_{dc}")
            ms_stats2 = c_st_done[("f2", th, NCB - 1)]

            def ln2_rows(e):
                e.tensor_scalar_mul(row_rstd[:, 0:TH], big[2][0:1, 0:TH],
                                    cons(CONS_INV2048)[0:1, :])
                e.drain()
                e.tensor_scalar_mul(row_mr[:, 0:TH], big[3][0:1, 0:TH],
                                    cons(CONS_INV2048)[0:1, :])
                e.drain()
                e.tensor_mul(big[2][32:33, 0:TH], row_rstd[:, 0:TH],
                             row_rstd[:, 0:TH])
                e.drain()
                e.tensor_sub(big[3][32:33, 0:TH], row_mr[:, 0:TH],
                             big[2][32:33, 0:TH])
                e.drain()
                return e.tensor_scalar(big[3][64:65, 0:TH], big[3][32:33, 0:TH],
                                       cons(CONS_EPS)[0:1, :], None, ALU.add)
            ms_lnr2 = sch.op("vector", ln2_rows, wait=[("TS", ms_stats2)],
                             inc=("VS", 1), tag=f"ln2rows{th}")

            def ln2_rstd(e):
                e.activation(big[2][64:65, 0:TH], big[3][64:65, 0:TH], ACT.Ln)
                e.drain()
                return e.activation(big[2][32:33, 0:TH], big[2][64:65, 0:TH],
                                    ACT.Exp, scale=cons(CONS_NHALF)[0:1, :])
            ms_rstd2 = sch.op("scalar", ln2_rstd, wait=[("VS", ms_lnr2)],
                              inc=("SS", 1), tag=f"ln2rstd{th}")

            def ln2_mr(e):
                e.tensor_mul(row_mr[:, 0:TH], row_rstd[:, 0:TH],
                             big[2][32:33, 0:TH])
                e.drain()
                return e.tensor_copy(row_rstd[:, 0:TH], big[2][32:33, 0:TH])
            ms_mr2 = sch.op("vector", ln2_mr,
                            wait=[("SS", ms_rstd2)], inc=("VS", 1), tag=f"ln2mr{th}")
            sch.mm("tensor",
                   lambda e: e.matmul(big[2][:, 0:TH], t_onerow[0:1, :],
                                      row_rstd[:, 0:TH], start=True, stop=True),
                   wait=[("VS", ms_mr2)], tag=f"bc3_{th}")
            bc2_ms = sch.mm("tensor",
                            lambda e: e.matmul(big[3][:, 0:TH], t_onerow[0:1, :],
                                               row_mr[:, 0:TH],
                                               start=True, stop=True),
                            inc=("TS", 1), tag=f"bc4_{th}")
            last_ap = None
            for dc in range(NCB):
                def vap2(e, dc=dc):
                    e.tensor_mul(ln2s[:, :], out2[:, dc, :], big[2][:, 0:TH])
                    e.drain()
                    return e.tensor_sub(out2[:, dc, :], ln2s[:, :], big[3][:, 0:TH])
                last_ap = sch.op("vector", vap2,
                                 wait=[("TS", bc2_ms)] if dc == 0 else (),
                                 inc=("VS", 1), tag=f"ap2_{th}_{dc}")
                out_st.append(sch.dma(
                    "sync",
                    lambda e, dc=dc, th=th: e.dma_start(
                        out=d_out[128 * dc:128 * (dc + 1), TH * th:TH * (th + 1)],
                        in_=out2[:, dc, :]),
                    wait=[("VS", last_ap)], sem="SO", tag=f"ost{th}_{dc}"))
            prev_f2_v = last_ap
            prev_sh_last = s_h[NFB - 1]
            prev_f2_ts = f2_done[NCB - 1]

        if stage == "cf":
            dbgq1 = nc.dram_tensor("dbgq1", [NFB, C, 128], BF16, kind="ExternalOutput").ap()
            dbgq2 = nc.dram_tensor("dbgq2", [NFB, C, 128], BF16, kind="ExternalOutput").ap()
            sch.dma("gpsimd", lambda e: e.dma_start(out=dbgq1, in_=d_q1all),
                    wait=[("SO", out_st[-1])], sem="DB", tag="dbgq1")
            sch.dma("gpsimd", lambda e: e.dma_start(out=dbgq2, in_=d_q2all),
                    sem="DB", tag="dbgq2")
            dbgh = nc.dram_tensor("dbgh", [FF, TH], F32, kind="ExternalOutput").ap()
            for fc in range(NFB):
                sch.dma("gpsimd", lambda e, fc=fc: e.dma_start(
                    out=dbgh[128 * fc:128 * (fc + 1), :], in_=hbuf[:, fc, :]),
                    wait=[("SO", out_st[-1])], sem="DB", tag=f"dbgh{fc}")
            for j in range(NCB):
                sch.dma("gpsimd", lambda e, j=j: e.dma_start(
                    out=dbg[0][128 * j:128 * (j + 1), :], in_=bfA[:, j, :]),
                    wait=[("SO", out_st[-1])], sem="DB", tag=f"dbg0_{j}")
                sch.dma("gpsimd", lambda e, j=j: e.dma_start(
                    out=dbg[1][128 * j:128 * (j + 1), :], in_=bfB[:, j, :]),
                    sem="DB", tag=f"dbg1_{j}")
            sch.op("gpsimd", lambda e: e.nop(), wait=[("DB", cnt["DB"])],
                   drain=False, tag="dbg_wait")

        sch.op("sync", lambda e: e.nop(), wait=[("SO", out_st[-1])],
               drain=False, tag="final")

        _finish(nc, sch, ctx)
    return nc


def _finish(nc, sch, ctx):
    sems = {}
    names = sorted({s for ops in sch.ops.values() for (w, _, i, _, _) in ops
                    for s in ([x[0] for x in w] + ([i[0]] if i else []))})
    for n in names:
        sems[n] = ctx.enter_context(nc.semaphore(n))
    block = ctx.enter_context(nc.Block())

    def emit(engname):
        def run(eng):
            for wait, fn, inc, drain, tag in sch.ops[engname]:
                for s, v in wait:
                    eng.wait_ge(sems[s], v)
                inst = fn(eng)
                if drain:
                    d = eng.drain()
                    if inc is not None:
                        d.then_inc(sems[inc[0]], inc[1])
                elif inc is not None:
                    inst.then_inc(sems[inc[0]], inc[1])
        return run

    block.sync(emit("sync"))
    block.gpsimd(emit("gpsimd"))
    block.tensor(emit("tensor"))
    block.vector(emit("vector"))
    block.scalar(emit("scalar"))


def _debug_ab(sch, stage, dbg, NUM, DEN, t_sums, t_acol, t_bcol,
              ms_A_done, ms_B_done, ms_horner, cnt):
    ms = ms_A_done if stage == "a" else ms_B_done
    for j in range(NCB):
        sch.dma("gpsimd", lambda e, j=j: e.dma_start(
            out=dbg[0][128 * j:128 * (j + 1), :], in_=NUM[:, j, :]),
            wait=[("VS", ms)], sem="DB", tag=f"dbg0_{j}")
        sch.dma("gpsimd", lambda e, j=j: e.dma_start(
            out=dbg[1][128 * j:128 * (j + 1), :], in_=DEN[:, j, :]),
            sem="DB", tag=f"dbg1_{j}")
    sch.dma("gpsimd", lambda e: e.dma_start(out=dbg["s"][:, 0:32], in_=t_sums[:, :]),
            sem="DB", tag="dbgs")
    if stage == "b":
        sch.dma("gpsimd", lambda e: e.dma_start(out=dbg["s"][:, 32:48],
                                                in_=t_acol[:, :]),
                wait=[("VS", ms_horner)], sem="DB", tag="dbga")
        sch.dma("gpsimd", lambda e: e.dma_start(out=dbg["s"][:, 48:64],
                                                in_=t_bcol[:, :]),
                sem="DB", tag="dbgb")
    sch.op("gpsimd", lambda e: e.nop(), wait=[("DB", cnt["DB"])],
           drain=False, tag="dbg_wait")


def _debug_c(sch, dbg, bfA, bfB, ms_C_done, cnt):
    for j in range(NCB):
        sch.dma("gpsimd", lambda e, j=j: e.dma_start(
            out=dbg[0][128 * j:128 * (j + 1), :], in_=bfA[:, j, :]),
            wait=[("VS", ms_C_done)], sem="DB", tag=f"dbg0_{j}")
        sch.dma("gpsimd", lambda e, j=j: e.dma_start(
            out=dbg[1][128 * j:128 * (j + 1), :], in_=bfB[:, j, :]),
            sem="DB", tag=f"dbg1_{j}")
    sch.op("gpsimd", lambda e: e.nop(), wait=[("DB", cnt["DB"])],
           drain=False, tag="dbg_wait")



def _blocks(wT, nb):
    K, M = wT.shape
    assert M == nb * 128 and K % 128 == 0
    nj = K // 128
    b = wT.reshape(nj, 128, nb, 128)
    b = b.transpose(2, 1, 0, 3)
    return np.ascontiguousarray(b.reshape(nb, 128, K))


def _cols(v):
    return np.ascontiguousarray(np.asarray(v, dtype=np.float32).reshape(-1, 128).T)


def make_in_maps(inputs):
    x = np.ascontiguousarray(np.asarray(inputs["L0_attn_x"], dtype=np.float32)[0])
    Wr = np.asarray(inputs["L0_attn_Wr"], dtype=np.float32)
    Wk = np.asarray(inputs["L0_attn_Wk"], dtype=np.float32)
    Wv = np.asarray(inputs["L0_attn_Wv"], dtype=np.float32)
    Wo = np.asarray(inputs["L0_attn_Wo"], dtype=np.float32)
    w1 = np.asarray(inputs["L0_ffn_w1"], dtype=np.float32)
    w2 = np.asarray(inputs["L0_ffn_w2"], dtype=np.float32)

    wkb = _blocks(np.ascontiguousarray(Wk.T), NCB)
    wvb = _blocks(np.ascontiguousarray(Wv.T), NCB)
    wrb = _blocks(np.ascontiguousarray(Wr.T), NCB)
    wob = _blocks(np.ascontiguousarray(Wo.T), NCB)
    w1b = _blocks(np.ascontiguousarray(w1.T), NFB)
    w2tb = np.ascontiguousarray(w2.T).reshape(FF, NCB, 128)
    w2q = np.empty((NCB, 4, 128, 2048), dtype=np.float32)
    for dc in range(NCB):
        blk = w2tb[:, dc, :]
        q = blk.reshape(4, 16, 128, 128)
        w2q[dc] = q.transpose(0, 2, 1, 3).reshape(4, 128, 2048)

    muk = _cols(inputs["L0_attn_mu_k"])
    mur = _cols(inputs["L0_attn_mu_r"])
    muv = _cols(inputs["L0_attn_mu_v"])
    uv = _cols(inputs["L0_attn_wkv_u"])
    wv_ = _cols(inputs["L0_attn_wkv_w"])
    b1 = _cols(inputs["L0_ffn_b1"])
    b2 = _cols(inputs["L0_ffn_b2"])

    in_maps = []
    for c in range(NCORES):
        sl = slice(TC * c, TC * (c + 1))
        xc = np.ascontiguousarray(x[sl].T)
        if c == 0:
            xp = np.zeros((128, NCB), dtype=np.float32)
        else:
            xp = _cols(x[TC * c - 1])
        mask = np.zeros(NCORES, dtype=np.float32)
        mask[:c] = 1.0
        maskb = np.ascontiguousarray(np.broadcast_to(mask, (128, NCORES)))
        w1s = np.ascontiguousarray(w1b[8 * c:8 * (c + 1)])
        w2s = np.ascontiguousarray(w2q[2 * c:2 * (c + 1)].reshape(8, 128, 2048))
        in_maps.append({
            "x": xc, "xprev": xp, "wkb": wkb, "wvb": wvb, "wrb": wrb,
            "wob": wob, "w1s": w1s, "w2s": w2s, "muk": muk, "mur": mur,
            "muv": muv, "uv": uv, "wv": wv_, "b1": b1, "b2": b2,
            "mask": maskb,
        })
    return in_maps


def kernel(**inputs):
    in_maps = make_in_maps(inputs)
    nc = build_program("full")
    res = run_bass_kernel_spmd(nc, in_maps, core_ids=list(range(NCORES)))
    out = np.empty((NCORES * TC, C), dtype=np.float32)
    for c in range(NCORES):
        out[TC * c:TC * (c + 1), :] = res.results[c]["out"].T
    return out.reshape(1, NCORES * TC, C)
